# version 08
# baseline (speedup 1.0000x reference)
import os
import sys
import numpy as np
import ml_dtypes
import concourse.bass as bass
import concourse.mybir as mybir
from concourse.bass_utils import run_bass_kernel_spmd
from contextlib import ExitStack

F32 = mybir.dt.float32
BF16 = mybir.dt.bfloat16
U8 = mybir.dt.uint8
AF = mybir.ActivationFunctionType
ALU = mybir.AluOpType
AXX = mybir.AxisListType.X

B, T, C = 4, 4096, 1024
TM = 2048
NCH = 8
H = 4
E1 = 257
NCK = 16
CP = 15.5
NE = 8


class Q:
    ENGS = ("tensor", "vector", "scalar", "sync", "gpsimd")

    def __init__(self):
        self.q = {e: [] for e in self.ENGS}
        self.cnt = {}

    def wait(self, eng, tok):
        if tok is not None:
            self.q[eng].append(("w", tok[0], tok[1]))

    def emit(self, eng, fn, inc=None, amt=1):
        self.q[eng].append(("e", fn, inc, amt))
        if inc is not None:
            self.cnt[inc] = self.cnt.get(inc, 0) + amt
            return (inc, self.cnt[inc])
        return None

    def tok(self, sem):
        return (sem, self.cnt.get(sem, 0))


def build_program():
    nc = bass.Bass("TRN2", target_bir_lowering=False, debug=False)

    xm = nc.dram_tensor("xm", [TM, C], F32, kind="ExternalInput").ap()
    xp = nc.dram_tensor("xp", [TM, C], F32, kind="ExternalInput").ap()
    wts = {}
    for nm in ("wq", "wk", "wv", "wg", "wo"):
        wts[nm] = nc.dram_tensor(nm, [C, C], F32, kind="ExternalInput").ap()
    w1h = nc.dram_tensor("w1", [NE, C, C], F32, kind="ExternalInput").ap()
    w2h = nc.dram_tensor("w2", [NE, C, C], F32, kind="ExternalInput").ap()
    td8h = nc.dram_tensor("td8", [128, 8], F32, kind="ExternalInput").ap()
    nsc8h = nc.dram_tensor("nsc8", [128, 8], F32, kind="ExternalInput").ap()
    mix8h = nc.dram_tensor("mix8", [128, 32], F32, kind="ExternalInput").ap()
    gw8h = nc.dram_tensor("gw8", [128, 64], F32, kind="ExternalInput").ap()
    gbrh = nc.dram_tensor("gbr", [128, 8], F32, kind="ExternalInput").ap()
    b1ch = nc.dram_tensor("b1c", [128, 64], F32, kind="ExternalInput").ap()
    b2h = nc.dram_tensor("b2", [8, C], F32, kind="ExternalInput").ap()
    identh = nc.dram_tensor("ident", [128, 128], BF16, kind="ExternalInput").ap()
    trilh = nc.dram_tensor("tril", [128, 128], U8, kind="ExternalInput").ap()
    patsh = nc.dram_tensor("pats", [128, 512], F32, kind="ExternalInput").ap()
    pmvh = nc.dram_tensor("pmv", [128, 1], F32, kind="ExternalInput").ap()
    outh = nc.dram_tensor("out", [TM, C], F32, kind="ExternalOutput").ap()
    xnf = nc.dram_tensor("xnf", [NCH, 128, 2 * TM + 1], BF16, kind="Internal").ap()
    g2d = nc.dram_tensor("g2d", [NCK, 128, C], BF16, kind="Internal").ap()
    MKDBG = bool(os.environ.get("MK_DBG"))
    dbg1 = nc.dram_tensor("dbg1", [128, 2 * TM + 1], BF16,
                          kind="ExternalOutput").ap() if MKDBG else None
    if MKDBG:
        dbg2 = nc.dram_tensor("dbg2", [128, C], BF16, kind="ExternalOutput").ap()
        dbg3 = nc.dram_tensor("dbg3", [128, 64], F32, kind="ExternalOutput").ap()
        dbg4 = nc.dram_tensor("dbg4", [128, 4 * 514], F32,
                              kind="ExternalOutput").ap()
        dbg5 = nc.dram_tensor("dbg5", [128, 4096], BF16,
                              kind="ExternalOutput").ap()
        dbg6 = nc.dram_tensor("dbg6", [128, 4096], BF16,
                              kind="ExternalOutput").ap()
        dbg7 = nc.dram_tensor("dbg7", [128, 1024], BF16,
                              kind="ExternalOutput").ap()
        dbg8 = nc.dram_tensor("dbg8", [128, 4096], BF16,
                              kind="ExternalOutput").ap()
        dbg9 = nc.dram_tensor("dbg9", [128, 4096], BF16,
                              kind="ExternalOutput").ap()
        dbgA = nc.dram_tensor("dbgA", [128, 4096], BF16,
                              kind="ExternalOutput").ap()
        dbgB = nc.dram_tensor("dbgB", [128, 16], F32, kind="ExternalOutput").ap()
        dbgC = nc.dram_tensor("dbgC", [4, 128, 2056], F32,
                              kind="ExternalOutput").ap()

    q = Q()
    tk = {}
    st = ExitStack()
    with st:
        ec = st.enter_context

        ident = ec(nc.sbuf_tensor("ident_s", [128, 128], BF16))
        tril = ec(nc.sbuf_tensor("tril_s", [128, 128], U8))
        cst = ec(nc.sbuf_tensor("cst", [128, 10], F32))
        mix8 = ec(nc.sbuf_tensor("mix8_s", [128, 32], F32))
        td8 = ec(nc.sbuf_tensor("td8_s", [128, 8], F32))
        nsc8 = ec(nc.sbuf_tensor("nsc8_s", [128, 8], F32))
        onesc = ec(nc.sbuf_tensor("onesc", [128, 1], BF16))
        onesr = ec(nc.sbuf_tensor("onesr", [1, 128], BF16))
        pmc = ec(nc.sbuf_tensor("pmc", [128, 1], BF16))
        zer = ec(nc.sbuf_tensor("zer", [128, 128], BF16))

        def ld(dst, src):
            return lambda e, d=dst, s=src: e.dma_start(d, s)

        for dst, src in ((ident[:], identh[:]), (tril[:], trilh[:]),
                         (mix8[:], mix8h[:]), (td8[:], td8h[:]),
                         (nsc8[:], nsc8h[:]), (pmc[:], pmvh[:])):
            q.emit("gpsimd", ld(dst, src), "G", 16)
        t_cdma = q.tok("G")
        for col, val in ((0, 1e-16), (1, -0.5), (2, 1.0 / 1024.0), (3, -1e30),
                         (4, 1e25), (5, -1.0), (6, 128.0), (7, 1e-5),
                         (8, -1.0 / 1024.0)):
            q.emit("gpsimd", lambda e, c=col, v=val: e.memset(cst[:, c:c + 1], v))
        q.emit("gpsimd", lambda e: e.memset(onesc[:], 1.0))
        q.emit("gpsimd", lambda e: e.memset(onesr[:], 1.0))
        t_cmem = q.emit("gpsimd", lambda e: e.memset(zer[:], 0.0), "GM", 1)
        for eng in ("tensor", "vector", "scalar"):
            q.wait(eng, t_cdma)
            q.wait(eng, t_cmem)

        pp = [ec(nc.psum_tensor(f"pp{i}", [128, 512], F32)) for i in range(2)]
        sc_ps = ec(nc.psum_tensor("sc_ps", [128, 512], F32))
        ps2 = [sc_ps, sc_ps]
        ps45 = [ec(nc.psum_tensor(f"ps45{i}", [128, 512], F32)) for i in range(2)]
        ptb = ec(nc.psum_tensor("ptb", [128, 128], F32))
        pt6b = ec(nc.psum_tensor("pt6b", [128, 128], BF16))
        pt7b = ec(nc.psum_tensor("pt7b", [128, 128], BF16))

        p1 = ExitStack()
        pc = p1.enter_context
        x_t = [pc(nc.sbuf_tensor(f"x_t{i}", [128, C], F32)) for i in range(2)]
        sqb0 = pc(nc.sbuf_tensor("sqb0", [128, C], F32))
        sqb = [sqb0, sqb0]
        xn_t = [pc(nc.sbuf_tensor(f"xn_t{i}", [128, C], BF16)) for i in range(2)]
        ssum = pc(nc.sbuf_tensor("ssum", [128, 16], F32))
        ssq = pc(nc.sbuf_tensor("ssq", [128, 16], F32))
        sa = pc(nc.sbuf_tensor("sa", [128, 16], F32))
        sb_ = pc(nc.sbuf_tensor("sb_", [128, 16], F32))
        stmp = pc(nc.sbuf_tensor("stmp", [128, 16], F32))
        xnbf = pc(nc.sbuf_tensor("xnbf", [128, NCH * 2049], BF16))
        w1s = [pc(nc.sbuf_tensor(f"w1s{i}", [128, NCH * 1024], BF16))
               for i in range(2)]
        w2s = [pc(nc.sbuf_tensor(f"w2s{i}", [128, NCH * 1024], BF16))
               for i in range(2)]
        h_sb = [pc(nc.sbuf_tensor(f"hsb{i}", [128, 8 * 512], BF16))
                for i in range(2)]
        out_sb = pc(nc.sbuf_tensor("out_sb", [128, NCK * 1024], F32))
        scores = pc(nc.sbuf_tensor("scores", [128, NCK * 8], F32))
        cge1 = pc(nc.sbuf_tensor("cge1", [128, NCK * 8], F32))
        cs2 = pc(nc.sbuf_tensor("cs2", [128, NCK * 8], F32))
        cm1 = pc(nc.sbuf_tensor("cm1", [128, NCK], F32))
        cm2 = pc(nc.sbuf_tensor("cm2", [128, NCK], F32))
        comb = pc(nc.sbuf_tensor("comb", [128, NCK * 8], F32))
        combbf = pc(nc.sbuf_tensor("combbf", [128, NCK * 8], BF16))
        combT = pc(nc.sbuf_tensor("combT", [8, TM], BF16))
        b2s = pc(nc.sbuf_tensor("b2s_s", [8, C], BF16))
        gws = pc(nc.sbuf_tensor("gws", [128, 64], BF16))
        gbr = pc(nc.sbuf_tensor("gbr_s", [128, 8], F32))
        b1c = pc(nc.sbuf_tensor("b1c_s", [128, 64], F32))
        g2t = [pc(nc.sbuf_tensor(f"g2t{i}", [128, C], BF16)) for i in range(2)]

        for dst, src in ((gws[:], gw8h[:]), (gbr[:], gbrh[:]),
                         (b1c[:], b1ch[:]), (b2s[:], b2h[:])):
            q.emit("gpsimd", ld(dst, src), "G", 16)
        t_ffnc = q.tok("G")
        t_halo = q.emit("gpsimd", lambda e: e.memset(
            xnbf[:].rearrange("p (c w) -> p c w", w=2049)[:, :, 0:1], 0.0), "GM", 1)

        def emit_ln_region(region, src_ap):
            q.wait("gpsimd", tk.get(("stats_done_v",)))
            q.wait("gpsimd", tk.get(("stats_done_s",)))
            tz = q.emit("gpsimd", lambda e: e.memset(ssq[:], 0.0), "GM", 1)
            q.wait("scalar", tz)
            for k in range(NCK):
                bi = k % 2
                xsf = tk.get(("xstat_free", bi))
                if xsf is not None:
                    q.wait("gpsimd", xsf[0])
                    q.wait("gpsimd", xsf[1])
                q.wait("gpsimd", tk.get(("xn_free", bi)))
                tl = q.emit("gpsimd", ld(x_t[bi][:],
                                         src_ap[128 * k:128 * (k + 1), :]),
                            "G", 16)
                q.wait("scalar", tl)
                q.wait("scalar", tk.get(("sq_free",)))
                ts = q.emit("scalar", lambda e, b=bi, kk=k: e.activation(
                    sqb[b][:], x_t[b][:], AF.Square,
                    accum_out=ssq[:, kk:kk + 1]), "S", 1)
                tk[("sq_free",)] = ts
                q.wait("vector", tl)
                tv = q.emit("vector", lambda e, b=bi, kk=k: e.reduce_sum(
                    ssum[:, kk:kk + 1], x_t[b][:], axis=AXX), "V", 1)
                tk[("xstat_free", bi)] = (tv, ts)
            q.wait("vector", tk[("sq_free",)])
            q.emit("vector", lambda e: e.tensor_mul(stmp[:], ssum[:], ssum[:]))
            q.emit("vector", lambda e: e.tensor_scalar(
                ssum[:], ssum[:], cst[:, 2:3], None, ALU.mult))
            tv = q.emit("vector", lambda e: e.scalar_tensor_tensor(
                ssq[:], stmp[:], cst[:, 8:9], ssq[:], ALU.mult, ALU.add),
                "V", 1)
            tk[("stats_done_v",)] = tv
            q.wait("scalar", tv)
            q.emit("scalar", lambda e: e.activation(
                sa[:], ssq[:], AF.Ln, scale=cst[:, 2:3], bias=cst[:, 7:8]))
            ts = q.emit("scalar", lambda e: e.activation(
                sa[:], sa[:], AF.Exp, scale=cst[:, 1:2]), "S", 1)
            tk[("stats_done_s",)] = ts
            q.wait("vector", ts)
            tvb = q.emit("vector", lambda e: e.scalar_tensor_tensor(
                sb_[:], ssum[:], cst[:, 5:6], sa[:], ALU.mult, ALU.mult), "V", 1)
            for k in range(NCK):
                bi = k % 2
                q.wait("sync", tk.get(("xn_free", bi)))
                q.wait("sync", tvb if k < 2 else None)
                q.wait("sync", ts if k < 2 else None)
                tl = q.emit("sync", ld(x_t[bi][:],
                                       src_ap[128 * k:128 * (k + 1), :]), "D", 16)
                q.wait("scalar", tl)
                q.wait("scalar", tvb)
                q.wait("scalar", tk.get(("xnt_free", bi)))
                ts = q.emit("scalar", lambda e, b=bi, kk=k: e.activation(
                    xn_t[b][:], x_t[b][:], AF.Identity,
                    scale=sa[:, kk:kk + 1], bias=sb_[:, kk:kk + 1]), "S", 1)
                tk[("xn_free", bi)] = ts
                ttl = None
                for cc in range(NCH):
                    pb = pt6b if cc % 2 == 0 else pt7b
                    q.wait("tensor", ts)
                    q.wait("tensor", tk.get(("tb", cc % 2)))
                    ttl = q.emit("tensor", lambda e, b=bi, c=cc, p=pb: e.transpose(
                        p[:], xn_t[b][:, 128 * c:128 * (c + 1)], ident[:]), "T", 1)
                    q.wait("vector", ttl)
                    dst = 2049 * cc + 1 + 128 * k
                    tv2 = q.emit("vector", lambda e, p=pb, d=dst: e.tensor_copy(
                        xnbf[:, d:d + 128], p[:]), "V", 1)
                    tk[("tb", cc % 2)] = tv2
                    tk[("xnbf", region, k, cc)] = tv2
                tk[("xnt_free", bi)] = ttl
            for cc in range(NCH):
                for k in range(NCK):
                    q.wait("sync", tk[("xnbf", region, k, cc)])
                if region == 0:
                    q.wait("sync", t_halo)
                    td = q.emit("sync", ld(
                        xnf[cc, :, 0:2049],
                        xnbf[:, 2049 * cc:2049 * (cc + 1)]), "D", 16)
                else:
                    td = q.emit("sync", ld(
                        xnf[cc, :, 2049:2049 + TM],
                        xnbf[:, 2049 * cc + 1:2049 * (cc + 1)]), "D", 16)
                tk[("xnf_store", region, cc)] = td

        emit_ln_region(0, xp)
        if MKDBG:
            q.wait("sync", q.tok("V"))
            q.wait("sync", q.tok("S"))
            q.emit("sync", ld(dbg3[:, 0:16], ssum[:]), "O", 16)
            q.emit("sync", ld(dbg3[:, 16:32], ssq[:]), "O", 16)
            q.emit("sync", ld(dbg3[:, 32:48], sa[:]), "O", 16)
            dmp = q.emit("sync", ld(dbg3[:, 48:64], sb_[:]), "O", 16)
            q.wait("vector", dmp)
            q.wait("scalar", dmp)
            q.wait("gpsimd", dmp)
        for cc in range(NCH):
            q.wait("vector", tk[("xnf_store", 0, cc)])
        emit_ln_region(1, xm)
        t_xn_ready = q.tok("V")

        q.wait("tensor", t_ffnc)
        for k in range(NCK):
            q.wait("tensor", tk.get(("gate_evac", )))
            q.wait("tensor", tk[("xnbf", 1, k, NCH - 1)])
            tt = None
            for ci in range(NCH):
                tt = q.emit("tensor", lambda e, kk=k, c=ci: e.matmul(
                    ptb[:, 0:8],
                    xnbf[:, 2049 * c + 1 + 128 * kk:2049 * c + 1 + 128 * (kk + 1)],
                    gws[:, 8 * c:8 * (c + 1)],
                    start=(c == 0), stop=(c == NCH - 1)),
                    "T" if ci == NCH - 1 else None, 1)
            q.wait("vector", tt)
            tv = q.emit("vector", lambda e, kk=k: e.tensor_add(
                scores[:, 8 * kk:8 * (kk + 1)], ptb[:, 0:8], gbr[:]), "V", 1)
            tk[("gate_evac",)] = tv
        for k in range(NCK):
            q.emit("vector", lambda e, kk=k: e.reduce_max(
                cm1[:, kk:kk + 1], scores[:, 8 * kk:8 * (kk + 1)], axis=AXX))
        for k in range(NCK):
            q.emit("vector", lambda e, kk=k: e.tensor_scalar(
                cge1[:, 8 * kk:8 * (kk + 1)], scores[:, 8 * kk:8 * (kk + 1)],
                cm1[:, kk:kk + 1], None, ALU.is_ge))
        for k in range(NCK):
            q.emit("vector", lambda e, kk=k: e.scalar_tensor_tensor(
                cs2[:, 8 * kk:8 * (kk + 1)], cge1[:, 8 * kk:8 * (kk + 1)],
                cst[:, 3:4], scores[:, 8 * kk:8 * (kk + 1)], ALU.mult, ALU.add))
        for k in range(NCK):
            q.emit("vector", lambda e, kk=k: e.reduce_max(
                cm2[:, kk:kk + 1], cs2[:, 8 * kk:8 * (kk + 1)], axis=AXX))
        for k in range(NCK):
            q.emit("vector", lambda e, kk=k: e.tensor_scalar(
                cge1[:, 8 * kk:8 * (kk + 1)], scores[:, 8 * kk:8 * (kk + 1)],
                cm2[:, kk:kk + 1], None, ALU.is_ge))
        q.emit("vector", lambda e: e.tensor_mul(comb[:], scores[:], cge1[:]))
        tcomb = q.emit("vector", lambda e: e.tensor_copy(combbf[:], comb[:]),
                       "V", 1)
        for k in range(NCK):
            pb = pt6b if k % 2 == 0 else pt7b
            q.wait("tensor", tcomb)
            q.wait("tensor", tk.get(("tb", k % 2)))
            tt = q.emit("tensor", lambda e, kk=k, p=pb: e.transpose(
                p[0:8, 0:128], combbf[:, 8 * kk:8 * (kk + 1)], ident[:]), "T", 1)
            q.wait("vector", tt)
            tv = q.emit("vector", lambda e, kk=k, p=pb: e.tensor_copy(
                combT[:, 128 * kk:128 * (kk + 1)], p[0:8, 0:128]), "V", 1)
            tk[("tb", k % 2)] = tv
            tk[("combT", k)] = tv
        for k in range(NCK):
            for coh in range(2):
                bi = coh
                bank = ps45[bi]
                q.wait("tensor", tk[("combT", k)])
                q.wait("tensor", tk.get(("oinit", bi)))
                tt = q.emit("tensor", lambda e, kk=k, ch=coh, bk=bank: e.matmul(
                    bk[:, 0:512], combT[:, 128 * kk:128 * (kk + 1)],
                    b2s[:, 512 * ch:512 * (ch + 1)], start=True, stop=True),
                    "T", 1)
                q.wait("vector", tt)
                tv = q.emit("vector", lambda e, kk=k, ch=coh, bk=bank:
                            e.tensor_copy(
                    out_sb[:, 1024 * kk + 512 * ch:1024 * kk + 512 * (ch + 1)],
                    bk[:, 0:512]), "V", 1)
                tk[("oinit", bi)] = tv
        t_oinit = q.tok("V")

        def load_expert(e_idx, buf):
            tt = None
            for cc in range(NCH):
                q.emit("gpsimd", ld(
                    w1s[buf][:, 1024 * cc:1024 * (cc + 1)],
                    w1h[e_idx, 128 * cc:128 * (cc + 1), :]), "G", 16)
                tt = q.emit("gpsimd", ld(
                    w2s[buf][:, 1024 * cc:1024 * (cc + 1)],
                    w2h[e_idx, 128 * cc:128 * (cc + 1), :]), "G", 16)
            return tt

        wl = {0: load_expert(0, 0)}
        hq_banks = [pp[0], pp[1], sc_ps]
        for ei in range(NE):
            buf = ei % 2
            if ei + 1 < NE:
                q.wait("gpsimd", tk.get(("wfree", (ei + 1) % 2)))
                wl[ei + 1] = load_expert(ei + 1, (ei + 1) % 2)
            q.wait("tensor", wl[ei])
            for t4 in range(4):
                hbuf = (ei * 4 + t4) % 2
                if ei == 0:
                    q.wait("tensor", tk[("xnbf", 1, 4 * t4 + 3, NCH - 1)])
                q.wait("scalar", tk.get(("hsb_free", hbuf)))
                relu_last = None
                for hoc in range(NCH):
                    bk = hq_banks[hoc % 3]
                    q.wait("tensor", tk.get(("hbank", hoc % 3)))
                    tt = None
                    for ci in range(NCH):
                        tt = q.emit("tensor", lambda e, c=ci, hc=hoc, tt4=t4,
                                    b=buf, bb=bk: e.matmul(
                            bb[:, 0:512],
                            w1s[b][:, 1024 * c + 128 * hc:
                                     1024 * c + 128 * (hc + 1)],
                            xnbf[:, 2049 * c + 1 + 512 * tt4:
                                 2049 * c + 1 + 512 * (tt4 + 1)],
                            start=(c == 0), stop=(c == NCH - 1)),
                            "T" if ci == NCH - 1 else None, 1)
                    q.wait("scalar", tt)
                    ts = q.emit("scalar", lambda e, hc=hoc, hb=hbuf, ee=ei,
                                bb=bk: e.activation(
                        h_sb[hb][:, 512 * hc:512 * (hc + 1)], bb[:, 0:512],
                        AF.Relu, bias=b1c[:, 8 * ee + hc:8 * ee + hc + 1]),
                        "S", 1)
                    tk[("hbank", hoc % 3)] = ts
                    relu_last = ts
                for ts4 in range(4):
                    kk = 4 * t4 + ts4
                    for coh in range(2):
                        bi = coh
                        bank = ps45[bi]
                        q.wait("tensor", relu_last)
                        q.wait("tensor", tk.get(("oute", bi), t_oinit))
                        tt = None
                        for hoc in range(NCH):
                            tt = q.emit("tensor", lambda e, hc=hoc, tts=ts4,
                                        ch=coh, hb=hbuf, b=buf, bk=bank: e.matmul(
                                bk[:, 0:512],
                                h_sb[hb][:, 512 * hc + 128 * tts:
                                          512 * hc + 128 * (tts + 1)],
                                w2s[b][:, 1024 * hc + 512 * ch:
                                         1024 * hc + 512 * (ch + 1)],
                                start=(hc == 0), stop=(hc == NCH - 1)),
                                "T" if hoc == NCH - 1 else None, 1)
                        q.wait("vector", tt)
                        tv = q.emit("vector", lambda e, k2=kk, ch=coh, ee=ei,
                                    bk=bank: e.scalar_tensor_tensor(
                            out_sb[:, 1024 * k2 + 512 * ch:
                                     1024 * k2 + 512 * (ch + 1)],
                            bk[:, 0:512], comb[:, 8 * k2 + ee:8 * k2 + ee + 1],
                            out_sb[:, 1024 * k2 + 512 * ch:
                                     1024 * k2 + 512 * (ch + 1)],
                            ALU.mult, ALU.add), "V", 1)
                        tk[("oute", bi)] = tv
                tk[("hsb_free", hbuf)] = q.tok("T")
            tk[("wfree", buf)] = q.tok("T")
        t_ffn_done = q.tok("V")
        for k in range(NCK):
            bi = k % 2
            q.wait("scalar", t_ffn_done)
            q.wait("scalar", tk.get(("g2_free", bi)))
            ts = q.emit("scalar", lambda e, kk=k, b=bi: e.activation(
                g2t[b][:], out_sb[:, 1024 * kk:1024 * (kk + 1)], AF.Silu),
                "S", 1)
            q.wait("sync", ts)
            td = q.emit("sync", ld(g2d[k, :, :], g2t[bi][:]), "D", 16)
            tk[("g2_free", bi)] = td

        q.wait("sync", q.tok("D"))
        q.wait("gpsimd", q.tok("G"))
        q.wait("gpsimd", q.tok("D"))
        q.wait("tensor", tk.get(("tb", 0)))
        q.wait("tensor", tk.get(("tb", 1)))
        q.emit("tensor", lambda e: e.transpose(pt6b[:], zer[:], ident[:]), "B", 1)
        q.emit("vector", lambda e: e.tensor_copy(stmp[:, 0:1], cst[:, 0:1]),
               "B", 1)
        q.emit("scalar", lambda e: e.activation(
            stmp[:, 1:2], cst[:, 0:1], AF.Identity), "B", 1)
        q.emit("gpsimd", lambda e: e.memset(stmp[:, 2:3], 0.0), "B", 1)
        for eng in Q.ENGS:
            q.wait(eng, ("B", 4))
        p1.close()

        MK_PHASE = int(os.environ.get("MK_PHASE", "2"))
        MK_SBS = int(os.environ.get("MK_SBS", "8"))
        STG = float(os.environ.get("MK_STG", "99"))
        p2 = ExitStack()
        pc = p2.enter_context
        ws = {}
        for nm in ("wk", "wv", "wq", "wg", "wo"):
            ws[nm] = pc(nc.sbuf_tensor(f"{nm}s", [128, NCH * 1024], BF16))
        xnsb0 = pc(nc.sbuf_tensor("xnsb0", [128, NCH * 513], BF16))
        xnsb = [xnsb0, xnsb0]
        xqbf = pc(nc.sbuf_tensor("xqbf", [128, NCH * 512], BF16))
        dtmp = pc(nc.sbuf_tensor("dtmp", [128, 2 * 512], F32))
        projf = pc(nc.sbuf_tensor("projf", [128, NCH * 512], F32))
        qsq = pc(nc.sbuf_tensor("qsq", [128, NCH * 512], BF16))
        qexp = pc(nc.sbuf_tensor("qexp", [128, NCH * 512], BF16))
        kexp = pc(nc.sbuf_tensor("kexp", [128, NCH * 512], BF16))
        k2t = pc(nc.sbuf_tensor("k2t", [128, 4 * 1024], BF16))
        vsb = pc(nc.sbuf_tensor("vsb", [128, 4 * 1024], BF16))
        gsb = pc(nc.sbuf_tensor("gsb", [128, 4 * 1024], BF16))
        At = pc(nc.sbuf_tensor("At", [128, 4 * 514], F32))
        Abf = pc(nc.sbuf_tensor("Abf", [128, 4 * 514], BF16))
        tq2 = pc(nc.sbuf_tensor("tq2", [128, NCH * 128], BF16))
        tkt = pc(nc.sbuf_tensor("tkt", [128, NCH * 128], BF16))
        trt = pc(nc.sbuf_tensor("trt", [128, NCH * 128], BF16))
        tk2f = pc(nc.sbuf_tensor("tk2f", [128, 128], BF16))
        tk2t = pc(nc.sbuf_tensor("tk2t", [128, 1024], BF16))
        w128 = pc(nc.sbuf_tensor("w128", [128, 8], F32))
        lw = pc(nc.sbuf_tensor("lw", [128, 8], F32))
        pats = pc(nc.sbuf_tensor("pats_s", [128, 512], F32))
        q2b = pc(nc.sbuf_tensor("q2b", [128, 2 * 256], BF16))
        ktb = pc(nc.sbuf_tensor("ktb", [128, 2 * 256], BF16))
        qrb = pc(nc.sbuf_tensor("qrb", [128, 2 * 640], BF16))
        ptm = [pc(nc.sbuf_tensor(f"ptm{i}", [128, 128], BF16)) for i in range(2)]
        attk = pc(nc.sbuf_tensor("attk", [128, 1024], BF16))
        attf = attk
        rzc = pc(nc.sbuf_tensor("rzc", [128, 4], F32))
        ssr = pc(nc.sbuf_tensor("ssr", [1, 512], F32))
        rrow = pc(nc.sbuf_tensor("rrow", [1, 512], BF16))
        g2l0 = pc(nc.sbuf_tensor("g2l0", [128, C], BF16))
        xtl0 = pc(nc.sbuf_tensor("xtl0", [128, C], F32))
        otl0 = pc(nc.sbuf_tensor("otl0", [128, C], F32))
        g2l = [g2l0, g2l0]
        xtl = [xtl0, xtl0]
        otl = [otl0, otl0]
        tmpf = dtmp

        if MK_PHASE >= 2:
            q.emit("gpsimd", ld(pats[:], patsh[:]), "G", 16)
        t_pats = q.tok("G")
        for nm in ("wk", "wv", "wq", "wg", "wo"):
            if MK_PHASE >= 2:
                for cc in range(NCH):
                    q.emit("gpsimd", ld(
                        ws[nm][:, 1024 * cc:1024 * (cc + 1)],
                        wts[nm][128 * cc:128 * (cc + 1), :]), "G", 16)
            tk[("wld", nm)] = q.tok("G")

        if MK_PHASE < 2:
            ts = None
        ts = q.emit("scalar", lambda e: e.activation(lw[:], td8[:], AF.Exp),
                    "S", 1) if MK_PHASE >= 2 else None
        q.wait("vector", ts)
        tv = q.emit("vector", lambda e: e.tensor_scalar(
            lw[:], lw[:], cst[:, 5:6], None, ALU.mult), "V", 1)
        q.wait("scalar", tv)
        q.wait("scalar", t_pats)
        ts_last = None
        for cc in range(NCH):
            q.emit("scalar", lambda e, c=cc: e.activation(
                tq2[:, 128 * c:128 * (c + 1)], pats[:, 0:128], AF.Exp,
                scale=lw[:, c:c + 1]))
            q.emit("scalar", lambda e, c=cc: e.activation(
                tkt[:, 128 * c:128 * (c + 1)], pats[:, 128:256], AF.Exp,
                scale=lw[:, c:c + 1]))
            ts_last = q.emit("scalar", lambda e, c=cc: e.activation(
                trt[:, 128 * c:128 * (c + 1)], pats[:, 256:384], AF.Exp,
                scale=lw[:, c:c + 1]), "S", 1)
            ts2 = q.emit("scalar", lambda e, c=cc: e.activation(
                tk2f[:], pats[:, 384:512], AF.Exp, scale=lw[:, c:c + 1]), "S", 1)
            pb = pt6b if cc % 2 == 0 else pt7b
            q.wait("tensor", ts2)
            q.wait("tensor", tk.get(("tb", cc % 2)))
            tt = q.emit("tensor", lambda e, p=pb: e.transpose(
                p[:], tk2f[:], ident[:]), "T", 1)
            q.wait("vector", tt)
            tv = q.emit("vector", lambda e, c=cc, p=pb: e.tensor_copy(
                tk2t[:, 128 * c:128 * (c + 1)], p[:]), "V", 1)
            tk[("tb", cc % 2)] = tv
            q.wait("scalar", tt)
        t_w128 = q.emit("scalar", lambda e: e.activation(
            w128[:], lw[:], AF.Exp, scale=cst[:, 6:7]), "S", 1)
        q.wait("vector", ts_last)
        q.emit("vector", lambda e: e.tensor_scalar(
            tkt[:], tkt[:], cst[:, 4:5], None, ALU.min))
        t_tbl = q.emit("vector", lambda e: e.tensor_scalar(
            trt[:], trt[:], cst[:, 4:5], None, ALU.min), "V", 1)
        if os.environ.get("MK_SCRUB"):
            for buf in (qexp, kexp, k2t, vsb, gsb, attk, xqbf, projf, qsq,
                        q2b, ktb, qrb, rzc, ssr, rrow, Abf, dtmp, tmpf,
                        xnsb0, g2l0, xtl0, otl0):
                q.emit("gpsimd", lambda e, b=buf: e.memset(b[:], 0.0))
        tA0 = q.emit("gpsimd", lambda e: e.memset(At[:], 0.0), "GM", 1)
        q.wait("vector", tA0)
        q.wait("sync", q.tok("GM"))

        proj_specs = {
            "k": ("wk", True, 1),
            "v": ("wv", False, 2),
            "q": ("wq", True, 0),
            "g": ("wg", False, 3),
        }

        sc_guard = None
        sel_guard = None
        ad_guard = None
        last_pt_tok = {}
        abf_free = None
        t_sc = None
        t_epi = {}
        xnsb_free = {}
        t_prev_sb_T = None
        t_exp_done = {}
        t_k2_done = None
        t_v_done = None
        t_g_done = None

        for s in range(MK_SBS if MK_PHASE >= 2 else 0):
            prefix = s < 4
            sbi = s % 2
            off = 512 * s if prefix else 2048 + 512 * (s - 4)
            q.wait("sync", xnsb_free.get(sbi))
            xl = None
            for cc in range(NCH):
                xl = q.emit("sync", ld(xnsb[sbi][:, 513 * cc:513 * (cc + 1)],
                                       xnf[cc, :, off:off + 513]), "D", 16)
            t_xn_sb = xl

            projs = ("k", "v") if prefix else ("k", "v", "q", "g")
            if STG < 2:
                projs = ()
            t_prev_mm = tk.get(("xqbf_free",))
            t_mix = None
            for pname in projs:
                wnm, fmaj, mixc = proj_specs[pname]
                q.wait("vector", t_xn_sb)
                q.wait("vector", t_prev_mm)
                for cc in range(NCH + 1):
                    if cc < NCH:
                        q.emit("vector", lambda e, c=cc, b=sbi: e.tensor_sub(
                            dtmp[:, 512 * (c % 2):512 * (c % 2) + 512],
                            xnsb[b][:, 513 * c + 1:513 * (c + 1)],
                            xnsb[b][:, 513 * c:513 * c + 512]))
                    if cc >= 1:
                        c0 = cc - 1
                        t_mix = q.emit("vector",
                                       lambda e, c=c0, b=sbi, m=mixc:
                                       e.scalar_tensor_tensor(
                            xqbf[:, 512 * c:512 * (c + 1)],
                            dtmp[:, 512 * (c % 2):512 * (c % 2) + 512],
                            mix8[:, 8 * m + c:8 * m + c + 1],
                            xnsb[b][:, 513 * c:513 * c + 512],
                            ALU.mult, ALU.add), "V", 1)
                q.wait("tensor", t_mix)
                q.wait("tensor", tk[("wld", wnm)])
                if STG < 3:
                    t_prev_mm = t_mix
                    continue
                if fmaj:
                    evd = {}
                    for co in range(NCH):
                        bi = co % 2
                        g_ = evd.get(bi)
                        if g_ is not None:
                            q.wait("tensor", g_[0])
                            q.wait("tensor", g_[1])
                        q.wait("tensor", t_epi.get(bi))
                        tt = None
                        for ci in range(NCH):
                            tt = q.emit("tensor", lambda e, c=ci, o=co, w=wnm:
                                        e.matmul(
                                pp[o % 2][:, 0:512],
                                ws[w][:, 1024 * c + 128 * o:
                                      1024 * c + 128 * (o + 1)],
                                xqbf[:, 512 * c:512 * (c + 1)],
                                start=(c == 0), stop=(c == NCH - 1)),
                                "T" if ci == NCH - 1 else None, 1)
                        if STG >= 3.25:
                            q.wait("vector", tt)
                            tv = q.emit("vector", lambda e, o=co: e.tensor_copy(
                                projf[:, 512 * o:512 * (o + 1)],
                                pp[o % 2][:, 0:512]), "V", 1)
                        else:
                            tv = tt
                        if STG >= 3.5:
                            q.wait("scalar", tv)
                            tsq = q.emit("scalar", lambda e, o=co: e.activation(
                                qsq[:, 512 * o:512 * (o + 1)],
                                projf[:, 512 * o:512 * (o + 1)],
                                AF.Square), "S", 1)
                        else:
                            tsq = tt
                        evd[bi] = (tv, tsq)
                    t_proj_T = q.tok("T")
                    if STG < 4:
                        t_prev_mm = t_proj_T
                        continue
                    q.wait("tensor", tsq)
                    q.wait("tensor", tv)
                    q.wait("tensor", tk.get(("ssfree",)))
                    q.wait("tensor", ad_guard)
                    tt = None
                    for ci in range(NCH):
                        tt = q.emit("tensor", lambda e, c=ci: e.matmul(
                            ps45[1][0:1, 0:512], onesc[:],
                            qsq[:, 512 * c:512 * (c + 1)],
                            start=(c == 0), stop=(c == NCH - 1)),
                            "T" if ci == NCH - 1 else None, 1)
                    q.wait("scalar", tt)
                    q.emit("scalar", lambda e: e.activation(
                        ssr[0:1, :], ps45[1][0:1, 0:512], AF.Ln,
                        scale=cst[0:1, 2:3], bias=cst[0:1, 0:1]))
                    ts = q.emit("scalar", lambda e: e.activation(
                        rrow[0:1, :], ssr[0:1, :], AF.Exp,
                        scale=cst[0:1, 1:2]), "S", 1)
                    q.wait("tensor", ts)
                    tt = q.emit("tensor", lambda e: e.matmul(
                        ps45[0][:, 0:512], onesr[:], rrow[0:1, :],
                        start=True, stop=True), "T", 1)
                    dst = qexp if pname == "q" else kexp
                    exp_toks = []
                    tv = None
                    for cc in range(NCH):
                        q.wait("vector", tt)
                        tv = q.emit("vector", lambda e, c=cc: e.tensor_mul(
                            projf[:, 512 * c:512 * (c + 1)],
                            projf[:, 512 * c:512 * (c + 1)],
                            ps45[0][:, 0:512]), "V", 1)
                        q.wait("scalar", tv)
                        ts = q.emit("scalar", lambda e, c=cc, d=dst: e.activation(
                            d[:, 512 * c:512 * (c + 1)],
                            projf[:, 512 * c:512 * (c + 1)], AF.Exp,
                            scale=nsc8[:, c:c + 1]), "S", 1)
                        exp_toks.append(ts)
                    tk[("ssfree",)] = tv
                    t_exp_done[pname] = exp_toks[-1]
                    if pname == "k" and STG >= 5:
                        ktr_pend = [(c4, cc) for c4 in range(4)
                                    for cc in range(NCH)]
                        ktr_exp = exp_toks
                else:
                    def emit_ktr(n):
                        for _ in range(n):
                            if not ktr_pend:
                                return
                            c4_, cc_ = ktr_pend.pop(0)
                            par_ = (c4_ * NCH + cc_) % 2
                            pb_ = pt6b if par_ == 0 else pt7b
                            q.wait("tensor", ktr_exp[cc_])
                            q.wait("tensor", tk.get(("tb", par_)))
                            tt_ = q.emit("tensor",
                                         lambda e, c=cc_, c4b=c4_, p=pb_:
                                         e.transpose(
                                p[:], kexp[:, 512 * c + 128 * c4b:
                                           512 * c + 128 * (c4b + 1)],
                                ident[:]), "T", 1)
                            q.wait("vector", tt_)
                            q.wait("vector", t_tbl)
                            tv_ = q.emit("vector",
                                         lambda e, c=cc_, c4b=c4_, p=pb_:
                                         e.tensor_mul(
                                k2t[:, 1024 * c4b + 128 * c:
                                    1024 * c4b + 128 * (c + 1)],
                                p[:], tk2t[:, 128 * c:128 * (c + 1)]),
                                "V", 1)
                            tk[("tb", par_)] = tv_
                            tk[("k2tok",)] = tv_
                    evd = {}
                    for ts4 in range(4):
                        for coh in range(2):
                            bi = (2 * ts4 + coh) % 2
                            if pname == "v":
                                emit_ktr(4)
                            q.wait("tensor", evd.get(bi))
                            q.wait("tensor", t_epi.get(bi))
                            tt = None
                            for ci in range(NCH):
                                tt = q.emit("tensor",
                                            lambda e, c=ci, t4_=ts4, ch=coh,
                                            w=wnm: e.matmul(
                                    pp[(2 * t4_ + ch) % 2][:, 0:512],
                                    xqbf[:, 512 * c + 128 * t4_:
                                         512 * c + 128 * (t4_ + 1)],
                                    ws[w][:, 1024 * c + 512 * ch:
                                          1024 * c + 512 * (ch + 1)],
                                    start=(c == 0), stop=(c == NCH - 1)),
                                    "T" if ci == NCH - 1 else None, 1)
                            if pname == "v":
                                q.wait("vector", tt)
                                tv = q.emit("vector", lambda e, t4_=ts4, ch=coh:
                                            e.tensor_copy(
                                    vsb[:, 1024 * t4_ + 512 * ch:
                                        1024 * t4_ + 512 * (ch + 1)],
                                    pp[(2 * t4_ + ch) % 2][:, 0:512]), "V", 1)
                                evd[bi] = tv
                            else:
                                q.wait("scalar", tt)
                                tv = q.emit("scalar", lambda e, t4_=ts4, ch=coh:
                                            e.activation(
                                    gsb[:, 1024 * t4_ + 512 * ch:
                                        1024 * t4_ + 512 * (ch + 1)],
                                    pp[(2 * t4_ + ch) % 2][:, 0:512],
                                    AF.Sigmoid), "S", 1)
                                evd[bi] = tv
                    if pname == "v":
                        emit_ktr(99)
                        t_k2_done = tk.get(("k2tok",))
                        t_v_done = q.tok("V")
                    else:
                        t_g_done = q.tok("S")
                    t_proj_T = q.tok("T")
                t_prev_mm = t_proj_T
            xnsb_free[sbi] = t_mix
            tk[("xqbf_free",)] = t_prev_mm
            if os.environ.get("MK_TRACE_A") and s < 4:
                pass

            maskcol = pmc if prefix else onesc
            qr_off = [0, 128, 256, 352, 448, 512, 576, 608]

            def emit_ttblock(c4_, h_):
                par_ = h_ % 2
                q.wait("vector", last_pt_tok.get(par_))
                for half, cc in ((0, 2 * h_), (1, 2 * h_ + 1)):
                    q.emit("vector", lambda e, hf=half, c=cc, cc4=c4_, p=par_:
                           e.tensor_mul(
                        q2b[:, 256 * p + 128 * hf:256 * p + 128 * (hf + 1)],
                        qexp[:, 512 * c + 128 * cc4:512 * c + 128 * (cc4 + 1)],
                        tq2[:, 128 * c:128 * (c + 1)]))
                    q.emit("vector", lambda e, hf=half, c=cc, cc4=c4_, p=par_:
                           e.tensor_mul(
                        ktb[:, 256 * p + 128 * hf:256 * p + 128 * (hf + 1)],
                        kexp[:, 512 * c + 128 * cc4:512 * c + 128 * (cc4 + 1)],
                        tkt[:, 128 * c:128 * (c + 1)]))
                tok = None
                for r_ in range(4):
                    j0_ = 32 * r_
                    for half, cc in ((0, 2 * h_), (1, 2 * h_ + 1)):
                        sl_ = 640 * par_ + qr_off[2 * r_ + half]
                        tok = q.emit("vector",
                                     lambda e, c=cc, cc4=c4_, jj=j0_, s0=sl_:
                                     e.tensor_mul(
                            qrb[:, s0:s0 + 128 - jj],
                            qexp[:, 512 * c + 128 * cc4 + jj:
                                 512 * c + 128 * (cc4 + 1)],
                            trt[:, 128 * c:128 * c + 128 - jj]), "V", 1)
                return tok

            for c4 in range(4):
                if not prefix:
                    q.wait("vector", t_exp_done["q"])
                    q.wait("vector", abf_free)
                    for hh in range(H):
                        tv_abf = q.emit("vector", lambda e, h2=hh: e.tensor_copy(
                            Abf[:, 514 * h2:514 * (h2 + 1)],
                            At[:, 514 * h2:514 * (h2 + 1)]), "V", 1)
                    t_abf = tv_abf
                    tok_tt = {0: emit_ttblock(c4, 0)}
                for h in range(H):
                    cc0, cc1 = 2 * h, 2 * h + 1
                    par = h % 2
                    sc_bank = sc_ps
                    if not prefix:
                        q.wait("tensor", tok_tt[h])
                        q.wait("tensor", sel_guard)
                        tmm = None
                        for r in range(4):
                            j0 = 32 * r
                            for half in range(2):
                                sl = 640 * par + qr_off[2 * r + half]
                                tmm = q.emit("tensor",
                                             lambda e, hf=half, jj=j0, p=par,
                                             s0=sl: e.matmul(
                                    ptb[jj:jj + 32, jj:128],
                                    ktb[:, 256 * p + 128 * hf + jj:
                                        256 * p + 128 * hf + jj + 32],
                                    qrb[:, s0:s0 + 128 - jj],
                                    start=(hf == 0), stop=(hf == 1),
                                    tile_position=(0, jj)),
                                    "T" if (r == 3 and half == 1) else None, 1)
                        last_pt_tok[par] = tmm
                        if h < 3:
                            tok_tt[h + 1] = emit_ttblock(c4, h + 1)
                        pmt = ptm[(c4 * H + h) % 2]
                        q.wait("vector", tmm)
                        q.wait("vector", tk.get(("ptmfree", (c4 * H + h) % 2)))
                        tv_sel = q.emit("vector", lambda e, pm_=pmt: e.select(
                            pm_[:], tril[:], ptb[:, 0:128], zer[:]), "V", 1)
                        sel_guard = tv_sel
                        q.wait("tensor", sc_guard)
                        q.wait("tensor", t_abf)
                        for half in range(2):
                            q.emit("tensor", lambda e, hf=half, h2=h,
                                   bk=sc_bank: e.matmul(
                                bk[:, 0:257],
                                q2b[:, 256 * (h2 % 2) + 128 * hf:
                                    256 * (h2 % 2) + 128 * (hf + 1)],
                                Abf[:, 514 * h2 + 257 * hf:
                                    514 * h2 + 257 * (hf + 1)],
                                start=(hf == 0), stop=False))
                        q.wait("tensor", tv_sel)
                        q.wait("tensor", t_v_done)
                        q.emit("tensor", lambda e, pm_=pmt, hh=h, c4_=c4,
                               bk=sc_bank: e.matmul(
                            bk[:, 0:256], pm_[:],
                            vsb[:, 1024 * c4_ + 256 * hh:
                                1024 * c4_ + 256 * (hh + 1)],
                            start=False, stop=False))
                        t_sc = q.emit("tensor", lambda e, pm_=pmt, bk=sc_bank:
                                      e.matmul(
                            bk[:, 256:257], pm_[:], onesc[:],
                            start=False, stop=True), "T", 1)
                        tk[("ptmfree", (c4 * H + h) % 2)] = t_sc
                    q.wait("tensor", t_k2_done)
                    q.wait("tensor", t_v_done)
                    q.wait("tensor", ad_guard)
                    q.wait("tensor", tk.get(("ssfree",)))
                    q.wait("tensor", t_exp_done["k" if prefix else "q"])
                    t_ad = None
                    for half, cc in ((0, cc0), (1, cc1)):
                        q.emit("tensor", lambda e, hf=half, c=cc, c4_=c4, hh=h:
                               e.matmul(
                            ps45[hf][:, 0:256],
                            k2t[:, 1024 * c4_ + 128 * c:
                                1024 * c4_ + 128 * (c + 1)],
                            vsb[:, 1024 * c4_ + 256 * hh:
                                1024 * c4_ + 256 * (hh + 1)],
                            start=True, stop=False))
                        t_ad = q.emit("tensor", lambda e, hf=half, c=cc, c4_=c4,
                                      mk=maskcol: e.matmul(
                            ps45[hf][:, 256:257],
                            k2t[:, 1024 * c4_ + 128 * c:
                                1024 * c4_ + 128 * (c + 1)],
                            mk[:], start=False, stop=True), "T", 1)
                    if not prefix:
                        q.wait("vector", t_sc)
                        q.emit("vector", lambda e, hh=h, bk=sc_bank:
                               e.reciprocal(rzc[:, hh:hh + 1], bk[:, 256:257]))
                    q.wait("vector", t_ad)
                    q.wait("vector", t_w128)
                    tv_au = None
                    for half in range(2):
                        tv_au = q.emit("vector", lambda e, hf=half, hh=h:
                                       e.scalar_tensor_tensor(
                            At[:, 514 * hh + 257 * hf:
                               514 * hh + 257 * (hf + 1)],
                            At[:, 514 * hh + 257 * hf:
                               514 * hh + 257 * (hf + 1)],
                            w128[:, 2 * hh + hf:2 * hh + hf + 1],
                            ps45[hf][:, 0:257], ALU.mult, ALU.add), "V", 1)
                    ad_guard = tv_au
                    if os.environ.get("MK_SERIAL"):
                        q.wait("tensor", tv_au)
                        q.wait("scalar", tv_au)
                        q.wait("gpsimd", tv_au)
                        q.wait("sync", tv_au)
                    if not prefix:
                        q.wait("vector", t_g_done)
                        q.wait("vector", tk.get(("attk_free",)))
                        tv_att = q.emit("vector", lambda e, hh=h, c4_=c4,
                                        bk=sc_bank: e.scalar_tensor_tensor(
                            attk[:, 256 * hh:256 * (hh + 1)], bk[:, 0:256],
                            rzc[:, hh:hh + 1],
                            gsb[:, 1024 * c4_ + 256 * hh:
                                1024 * c4_ + 256 * (hh + 1)],
                            ALU.mult, ALU.mult), "V", 1)
                        sc_guard = tv_att
                if not prefix:
                    abf_free = t_sc
                if not prefix:
                    mc = 4 * (s - 4) + c4
                    eb = c4 % 2
                    q.wait("sync", t_epi.get(("xfree", eb)))
                    tx = q.emit("sync", ld(xtl[eb][:],
                                           xm[128 * mc:128 * (mc + 1), :]),
                                "D", 16)
                    tg = q.emit("sync", ld(g2l[eb][:], g2d[mc, :, :]), "D", 16)
                    t_att_all = sc_guard
                    tt = None
                    for cc in range(NCH):
                        par2 = cc % 2
                        pb = pt6b if par2 == 0 else pt7b
                        q.wait("tensor", t_att_all)
                        q.wait("tensor", tk.get(("tb", par2)))
                        tt = q.emit("tensor", lambda e, c=cc, p=pb: e.transpose(
                            p[:], attk[:, 128 * c:128 * (c + 1)], ident[:]),
                            "T", 1)
                        q.wait("vector", tt)
                        tv = q.emit("vector", lambda e, c=cc, p=pb:
                                    e.tensor_copy(
                            attf[:, 128 * c:128 * (c + 1)], p[:]), "V", 1)
                        tk[("tb", par2)] = tv
                    t_attf = q.tok("V")
                    q.wait("tensor", t_attf)
                    q.wait("tensor", tk.get(("attf_free",)))
                    q.wait("tensor", t_epi.get(0))
                    q.wait("tensor", t_epi.get(1))
                    t_wo = None
                    for coh in range(2):
                        for ci in range(NCH):
                            t_wo = q.emit("tensor", lambda e, c=ci, ch=coh:
                                          e.matmul(
                                pp[ch][:, 0:512],
                                attf[:, 128 * c:128 * (c + 1)],
                                ws["wo"][:, 1024 * c + 512 * ch:
                                         1024 * c + 512 * (ch + 1)],
                                start=(c == 0), stop=(c == NCH - 1)),
                                "T" if ci == NCH - 1 else None, 1)
                    tk[("attf_free",)] = t_wo
                    tk[("attk_free",)] = t_wo
                    q.wait("vector", t_wo)
                    q.wait("vector", tx)
                    q.wait("vector", tg)
                    q.wait("vector", t_epi.get(("ofree", eb)))
                    q.emit("vector", lambda e, b=eb: e.tensor_mul(
                        tmpf[:, 0:512], pp[0][:, 0:512], g2l[b][:, 0:512]))
                    q.emit("vector", lambda e, b=eb: e.tensor_mul(
                        tmpf[:, 512:1024], pp[1][:, 0:512], g2l[b][:, 512:1024]))
                    q.emit("vector", lambda e, b=eb: e.tensor_add(
                        otl[b][:, 0:512], tmpf[:, 0:512], xtl[b][:, 0:512]))
                    tv = q.emit("vector", lambda e, b=eb: e.tensor_add(
                        otl[b][:, 512:1024], tmpf[:, 512:1024],
                        xtl[b][:, 512:1024]), "V", 1)
                    t_epi[0] = tv
                    t_epi[1] = tv
                    t_epi[("xfree", eb)] = tv
                    q.wait("sync", tv)
                    to = q.emit("sync", ld(outh[128 * mc:128 * (mc + 1), :],
                                           otl[eb][:]), "O", 16)
                    t_epi[("ofree", eb)] = to
            if os.environ.get("MK_TRACE_A") and s < 4:
                q.wait("sync", ad_guard)
                tda = q.emit("sync", lambda e, ss=s: e.dma_start(
                    dbgC[ss, :, :], At[:]), "O", 16)
                q.wait("tensor", tda)
                q.wait("vector", tda)
            t_prev_sb_T = q.tok("T")

        if MKDBG:
            q.emit("sync", ld(dbg1[:], xnf[0, :, :]), "O", 16)
            q.emit("sync", ld(dbg2[:], g2d[0, :, :]), "O", 16)
            q.wait("sync", q.tok("V"))
            q.wait("sync", q.tok("S"))
            q.wait("sync", q.tok("T"))
            q.emit("sync", lambda e: e.dma_start(dbg4[:], At[:]), "O", 16)
            q.emit("sync", lambda e: e.dma_start(dbg5[:], qexp[:]), "O", 16)
            q.emit("sync", lambda e: e.dma_start(dbg6[:], kexp[:]), "O", 16)
            q.emit("sync", lambda e: e.dma_start(dbg7[:], attk[:]), "O", 16)
            q.emit("sync", lambda e: e.dma_start(dbg8[:], vsb[:]), "O", 16)
            q.emit("sync", lambda e: e.dma_start(dbg9[:], gsb[:]), "O", 16)
            q.emit("sync", lambda e: e.dma_start(dbgA[:], k2t[:]), "O", 16)
            q.emit("sync", lambda e: e.dma_start(dbgB[:, 0:8], w128[:]), "O", 16)
            q.emit("sync", lambda e: e.dma_start(dbgB[:, 8:16], lw[:]), "O", 16)
        q.wait("sync", q.tok("O"))
        q.wait("gpsimd", q.tok("G"))

        sems = {}
        for s_ in ("T", "V", "S", "D", "G", "GM", "O", "B"):
            sems[s_] = ec(nc.semaphore(s_))
        block = ec(nc.Block())

        def runner(eng_name):
            def body(e):
                for item in q.q[eng_name]:
                    if item[0] == "w":
                        e.wait_ge(sems[item[1]], item[2])
                    else:
                        ins = item[1](e)
                        if item[2] is not None:
                            ins.then_inc(sems[item[2]], item[3])
            return body

        block.tensor(runner("tensor"))
        block.vector(runner("vector"))
        block.scalar(runner("scalar"))
        block.sync(runner("sync"))
        block.gpsimd(runner("gpsimd"))
        p2.close()

    return nc


_prog_cache = {}


def kernel(**inputs):
    x = np.asarray(inputs["L0_attn_x"], np.float32)
    td = np.asarray(inputs["L0_attn_time_decay"], np.float32)
    nsc = np.asarray(inputs["L0_attn_norm_scale"], np.float32)
    mixes = [np.asarray(inputs[f"L0_attn_mix_{m}"], np.float32).reshape(C)
             for m in "qkvg"]
    W = {m: np.asarray(inputs[f"L0_attn_W{m}"], np.float32) for m in "qkvgo"}
    gw = np.asarray(inputs["L0_ffn_gate_w"], np.float32)
    gb = np.asarray(inputs["L0_ffn_gate_b"], np.float32)
    w1 = np.asarray(inputs["L0_ffn_w1"], np.float32)
    b1 = np.asarray(inputs["L0_ffn_b1"], np.float32)
    w2 = np.asarray(inputs["L0_ffn_w2"], np.float32)
    b2 = np.asarray(inputs["L0_ffn_b2"], np.float32)

    col8 = lambda v: np.ascontiguousarray(v.reshape(8, 128).T)
    shared = {
        "wq": np.ascontiguousarray(W["q"].T),
        "wk": np.ascontiguousarray(W["k"].T),
        "wv": np.ascontiguousarray(W["v"].T),
        "wg": np.ascontiguousarray(W["g"].T),
        "wo": np.ascontiguousarray(W["o"].T),
        "w1": np.ascontiguousarray(w1), "w2": np.ascontiguousarray(w2),
        "td8": col8(td), "nsc8": col8(nsc),
        "mix8": np.ascontiguousarray(
            np.concatenate([col8(m) for m in mixes], axis=1)),
        "gw8": np.ascontiguousarray(
            gw.reshape(8, 128, 8).transpose(1, 0, 2).reshape(128, 64)),
        "gbr": np.ascontiguousarray(
            np.broadcast_to(gb[None, :], (128, 8))).astype(np.float32),
        "b1c": np.ascontiguousarray(
            b1.reshape(8, 8, 128).transpose(2, 0, 1).reshape(128, 64)),
        "b2": np.ascontiguousarray(b2),
        "ident": np.eye(128, dtype=ml_dtypes.bfloat16),
        "tril": (np.arange(128)[None, :] >= np.arange(128)[:, None]).astype(
            np.uint8),
    }
    i_idx = np.arange(128, dtype=np.float32)
    pats = np.stack([
        i_idx + 1.0,
        CP - (i_idx % 32),
        i_idx - CP,
        127.0 - i_idx,
    ], axis=0).reshape(1, 512)
    shared["pats"] = np.ascontiguousarray(
        np.broadcast_to(pats, (128, 512))).astype(np.float32)

    if "prog" not in _prog_cache:
        _prog_cache["prog"] = build_program()
    nc = _prog_cache["prog"]

    zeros_p = np.zeros((TM, C), np.float32)
    in_maps = []
    for core in range(8):
        b_, half = core // 2, core % 2
        m = dict(shared)
        m["xm"] = np.ascontiguousarray(x[b_, TM * half:TM * (half + 1)])
        m["xp"] = np.ascontiguousarray(x[b_, 0:TM]) if half == 1 else zeros_p
        m["pmv"] = np.full((128, 1), float(half), np.float32)
        in_maps.append(m)

    res = run_bass_kernel_spmd(nc, in_maps, core_ids=list(range(8)))
    out = np.empty((B, T, C), np.float32)
    for core in range(8):
        b_, half = core // 2, core % 2
        out[b_, TM * half:TM * (half + 1)] = res.results[core]["out"]
    if os.environ.get("MK_DBG"):
        sys.modules[__name__]._dbg = [
            {k2: res.results[c].get(k2) for k2 in
             ("dbg1", "dbg2", "dbg3", "dbg4", "dbg5", "dbg6", "dbg7",
              "dbg8", "dbg9", "dbgA", "dbgB", "dbgC")}
            for c in range(8)]
    return out


# version 10
# speedup vs baseline: 1.0012x; 1.0012x over previous
import os
import sys
import numpy as np
import ml_dtypes
import concourse.bass as bass
import concourse.mybir as mybir
from concourse.bass_utils import run_bass_kernel_spmd
from contextlib import ExitStack

F32 = mybir.dt.float32
BF16 = mybir.dt.bfloat16
U8 = mybir.dt.uint8
AF = mybir.ActivationFunctionType
ALU = mybir.AluOpType
AXX = mybir.AxisListType.X

B, T, C = 4, 4096, 1024
TM = 2048
NCH = 8
H = 4
E1 = 257
NCK = 16
CP = 15.5
NE = 8


class Q:
    ENGS = ("tensor", "vector", "scalar", "sync", "gpsimd")

    def __init__(self):
        self.q = {e: [] for e in self.ENGS}
        self.cnt = {}

    def wait(self, eng, tok):
        if tok is not None:
            self.q[eng].append(("w", tok[0], tok[1]))

    def emit(self, eng, fn, inc=None, amt=1):
        self.q[eng].append(("e", fn, inc, amt))
        if inc is not None:
            self.cnt[inc] = self.cnt.get(inc, 0) + amt
            return (inc, self.cnt[inc])
        return None

    def tok(self, sem):
        return (sem, self.cnt.get(sem, 0))


def build_program():
    nc = bass.Bass("TRN2", target_bir_lowering=False, debug=False)

    xm = nc.dram_tensor("xm", [TM, C], F32, kind="ExternalInput").ap()
    xp = nc.dram_tensor("xp", [TM, C], F32, kind="ExternalInput").ap()
    wts = {}
    for nm in ("wq", "wk", "wv", "wg", "wo"):
        wts[nm] = nc.dram_tensor(nm, [C, C], F32, kind="ExternalInput").ap()
    w1h = nc.dram_tensor("w1", [NE, C, C], F32, kind="ExternalInput").ap()
    w2h = nc.dram_tensor("w2", [NE, C, C], F32, kind="ExternalInput").ap()
    td8h = nc.dram_tensor("td8", [128, 8], F32, kind="ExternalInput").ap()
    nsc8h = nc.dram_tensor("nsc8", [128, 8], F32, kind="ExternalInput").ap()
    mix8h = nc.dram_tensor("mix8", [128, 32], F32, kind="ExternalInput").ap()
    gw8h = nc.dram_tensor("gw8", [128, 64], F32, kind="ExternalInput").ap()
    gbrh = nc.dram_tensor("gbr", [128, 8], F32, kind="ExternalInput").ap()
    b1ch = nc.dram_tensor("b1c", [128, 64], F32, kind="ExternalInput").ap()
    b2h = nc.dram_tensor("b2", [8, C], F32, kind="ExternalInput").ap()
    identh = nc.dram_tensor("ident", [128, 128], BF16, kind="ExternalInput").ap()
    trilh = nc.dram_tensor("tril", [128, 128], U8, kind="ExternalInput").ap()
    patsh = nc.dram_tensor("pats", [128, 512], F32, kind="ExternalInput").ap()
    pmvh = nc.dram_tensor("pmv", [128, 1], F32, kind="ExternalInput").ap()
    outh = nc.dram_tensor("out", [TM, C], F32, kind="ExternalOutput").ap()
    xnf = nc.dram_tensor("xnf", [NCH, 128, 2 * TM + 1], BF16, kind="Internal").ap()
    g2d = nc.dram_tensor("g2d", [NCK, 128, C], BF16, kind="Internal").ap()
    MKDBG = bool(os.environ.get("MK_DBG"))
    dbg1 = nc.dram_tensor("dbg1", [128, 2 * TM + 1], BF16,
                          kind="ExternalOutput").ap() if MKDBG else None
    if MKDBG:
        dbg2 = nc.dram_tensor("dbg2", [128, C], BF16, kind="ExternalOutput").ap()
        dbg3 = nc.dram_tensor("dbg3", [128, 64], F32, kind="ExternalOutput").ap()
        dbg4 = nc.dram_tensor("dbg4", [128, 4 * 514], F32,
                              kind="ExternalOutput").ap()
        dbg5 = nc.dram_tensor("dbg5", [128, 4096], BF16,
                              kind="ExternalOutput").ap()
        dbg6 = nc.dram_tensor("dbg6", [128, 4096], BF16,
                              kind="ExternalOutput").ap()
        dbg7 = nc.dram_tensor("dbg7", [128, 1024], BF16,
                              kind="ExternalOutput").ap()
        dbg8 = nc.dram_tensor("dbg8", [128, 4096], BF16,
                              kind="ExternalOutput").ap()
        dbg9 = nc.dram_tensor("dbg9", [128, 4096], BF16,
                              kind="ExternalOutput").ap()
        dbgA = nc.dram_tensor("dbgA", [128, 4096], BF16,
                              kind="ExternalOutput").ap()
        dbgB = nc.dram_tensor("dbgB", [128, 16], F32, kind="ExternalOutput").ap()
        dbgC = nc.dram_tensor("dbgC", [4, 128, 2056], F32,
                              kind="ExternalOutput").ap()

    q = Q()
    tk = {}
    st = ExitStack()
    with st:
        ec = st.enter_context

        ident = ec(nc.sbuf_tensor("ident_s", [128, 128], BF16))
        tril = ec(nc.sbuf_tensor("tril_s", [128, 128], U8))
        cst = ec(nc.sbuf_tensor("cst", [128, 10], F32))
        mix8 = ec(nc.sbuf_tensor("mix8_s", [128, 32], F32))
        td8 = ec(nc.sbuf_tensor("td8_s", [128, 8], F32))
        nsc8 = ec(nc.sbuf_tensor("nsc8_s", [128, 8], F32))
        onesc = ec(nc.sbuf_tensor("onesc", [128, 1], BF16))
        onesr = ec(nc.sbuf_tensor("onesr", [1, 128], BF16))
        pmc = ec(nc.sbuf_tensor("pmc", [128, 1], BF16))
        zer = ec(nc.sbuf_tensor("zer", [128, 128], BF16))

        def ld(dst, src):
            return lambda e, d=dst, s=src: e.dma_start(d, s)

        for dst, src in ((ident[:], identh[:]), (tril[:], trilh[:]),
                         (mix8[:], mix8h[:]), (td8[:], td8h[:]),
                         (nsc8[:], nsc8h[:]), (pmc[:], pmvh[:])):
            q.emit("gpsimd", ld(dst, src), "G", 16)
        t_cdma = q.tok("G")
        for col, val in ((0, 1e-16), (1, -0.5), (2, 1.0 / 1024.0), (3, -1e30),
                         (4, 1e25), (5, -1.0), (6, 128.0), (7, 1e-5),
                         (8, -1.0 / 1024.0)):
            q.emit("gpsimd", lambda e, c=col, v=val: e.memset(cst[:, c:c + 1], v))
        q.emit("gpsimd", lambda e: e.memset(onesc[:], 1.0))
        q.emit("gpsimd", lambda e: e.memset(onesr[:], 1.0))
        t_cmem = q.emit("gpsimd", lambda e: e.memset(zer[:], 0.0), "GM", 1)
        for eng in ("tensor", "vector", "scalar"):
            q.wait(eng, t_cdma)
            q.wait(eng, t_cmem)

        pp = [ec(nc.psum_tensor(f"pp{i}", [128, 512], F32)) for i in range(2)]
        sc_ps = ec(nc.psum_tensor("sc_ps", [128, 512], F32))
        ps2 = [sc_ps, sc_ps]
        ps45 = [ec(nc.psum_tensor(f"ps45{i}", [128, 512], F32)) for i in range(2)]
        ptb = ec(nc.psum_tensor("ptb", [128, 128], F32))
        pt6b = ec(nc.psum_tensor("pt6b", [128, 128], BF16))
        pt7b = ec(nc.psum_tensor("pt7b", [128, 128], BF16))

        p1 = ExitStack()
        pc = p1.enter_context
        x_t = [pc(nc.sbuf_tensor(f"x_t{i}", [128, C], F32)) for i in range(2)]
        sqb0 = pc(nc.sbuf_tensor("sqb0", [128, C], F32))
        sqb = [sqb0, sqb0]
        xn_t = [pc(nc.sbuf_tensor(f"xn_t{i}", [128, C], BF16)) for i in range(2)]
        ssum = pc(nc.sbuf_tensor("ssum", [128, 16], F32))
        ssq = pc(nc.sbuf_tensor("ssq", [128, 16], F32))
        sa = pc(nc.sbuf_tensor("sa", [128, 16], F32))
        sb_ = pc(nc.sbuf_tensor("sb_", [128, 16], F32))
        stmp = pc(nc.sbuf_tensor("stmp", [128, 16], F32))
        xnbf = pc(nc.sbuf_tensor("xnbf", [128, NCH * 2049], BF16))
        w1s = [pc(nc.sbuf_tensor(f"w1s{i}", [128, NCH * 1024], BF16))
               for i in range(2)]
        w2s = [pc(nc.sbuf_tensor(f"w2s{i}", [128, NCH * 1024], BF16))
               for i in range(2)]
        h_sb = [pc(nc.sbuf_tensor(f"hsb{i}", [128, 8 * 512], BF16))
                for i in range(2)]
        out_sb = pc(nc.sbuf_tensor("out_sb", [128, NCK * 1024], F32))
        scores = pc(nc.sbuf_tensor("scores", [128, NCK * 8], F32))
        cge1 = pc(nc.sbuf_tensor("cge1", [128, NCK * 8], F32))
        cs2 = pc(nc.sbuf_tensor("cs2", [128, NCK * 8], F32))
        cm1 = pc(nc.sbuf_tensor("cm1", [128, NCK], F32))
        cm2 = pc(nc.sbuf_tensor("cm2", [128, NCK], F32))
        comb = pc(nc.sbuf_tensor("comb", [128, NCK * 8], F32))
        combbf = pc(nc.sbuf_tensor("combbf", [128, NCK * 8], BF16))
        combT = pc(nc.sbuf_tensor("combT", [8, TM], BF16))
        b2s = pc(nc.sbuf_tensor("b2s_s", [8, C], BF16))
        gws = pc(nc.sbuf_tensor("gws", [128, 64], BF16))
        gbr = pc(nc.sbuf_tensor("gbr_s", [128, 8], F32))
        b1c = pc(nc.sbuf_tensor("b1c_s", [128, 64], F32))
        g2t = [pc(nc.sbuf_tensor(f"g2t{i}", [128, C], BF16)) for i in range(2)]

        for dst, src in ((gws[:], gw8h[:]), (gbr[:], gbrh[:]),
                         (b1c[:], b1ch[:]), (b2s[:], b2h[:])):
            q.emit("gpsimd", ld(dst, src), "G", 16)
        t_ffnc = q.tok("G")
        t_halo = q.emit("gpsimd", lambda e: e.memset(
            xnbf[:].rearrange("p (c w) -> p c w", w=2049)[:, :, 0:1], 0.0), "GM", 1)

        def emit_ln_region(region, src_ap):
            q.wait("gpsimd", tk.get(("stats_done_v",)))
            q.wait("gpsimd", tk.get(("stats_done_s",)))
            tz = q.emit("gpsimd", lambda e: e.memset(ssq[:], 0.0), "GM", 1)
            q.wait("scalar", tz)
            for k in range(NCK):
                bi = k % 2
                xsf = tk.get(("xstat_free", bi))
                if xsf is not None:
                    q.wait("sync", xsf[0])
                    q.wait("sync", xsf[1])
                q.wait("sync", tk.get(("xn_free", bi)))
                tl = q.emit("sync", ld(x_t[bi][:],
                                       src_ap[128 * k:128 * (k + 1), :]), "D", 16)
                q.wait("scalar", tl)
                q.wait("scalar", tk.get(("sq_free",)))
                ts = q.emit("scalar", lambda e, b=bi, kk=k: e.activation(
                    sqb[b][:], x_t[b][:], AF.Square,
                    accum_out=ssq[:, kk:kk + 1]), "S", 1)
                tk[("sq_free",)] = ts
                q.wait("vector", tl)
                tv = q.emit("vector", lambda e, b=bi, kk=k: e.reduce_sum(
                    ssum[:, kk:kk + 1], x_t[b][:], axis=AXX), "V", 1)
                tk[("xstat_free", bi)] = (tv, ts)
            q.wait("vector", tk[("sq_free",)])
            q.emit("vector", lambda e: e.tensor_mul(stmp[:], ssum[:], ssum[:]))
            q.emit("vector", lambda e: e.tensor_scalar(
                ssum[:], ssum[:], cst[:, 2:3], None, ALU.mult))
            tv = q.emit("vector", lambda e: e.scalar_tensor_tensor(
                ssq[:], stmp[:], cst[:, 8:9], ssq[:], ALU.mult, ALU.add),
                "V", 1)
            tk[("stats_done_v",)] = tv
            q.wait("scalar", tv)
            q.emit("scalar", lambda e: e.activation(
                sa[:], ssq[:], AF.Ln, scale=cst[:, 2:3], bias=cst[:, 7:8]))
            ts = q.emit("scalar", lambda e: e.activation(
                sa[:], sa[:], AF.Exp, scale=cst[:, 1:2]), "S", 1)
            tk[("stats_done_s",)] = ts
            q.wait("vector", ts)
            tvb = q.emit("vector", lambda e: e.scalar_tensor_tensor(
                sb_[:], ssum[:], cst[:, 5:6], sa[:], ALU.mult, ALU.mult), "V", 1)
            for k in range(NCK):
                bi = k % 2
                q.wait("sync", tk.get(("xn_free", bi)))
                q.wait("sync", tvb if k < 2 else None)
                q.wait("sync", ts if k < 2 else None)
                tl = q.emit("sync", ld(x_t[bi][:],
                                       src_ap[128 * k:128 * (k + 1), :]), "D", 16)
                q.wait("scalar", tl)
                q.wait("scalar", tvb)
                q.wait("scalar", tk.get(("xnt_free", bi)))
                ts = q.emit("scalar", lambda e, b=bi, kk=k: e.activation(
                    xn_t[b][:], x_t[b][:], AF.Identity,
                    scale=sa[:, kk:kk + 1], bias=sb_[:, kk:kk + 1]), "S", 1)
                tk[("xn_free", bi)] = ts
                ttl = None
                for cc in range(NCH):
                    pb = pt6b if cc % 2 == 0 else pt7b
                    q.wait("tensor", ts)
                    q.wait("tensor", tk.get(("tb", cc % 2)))
                    ttl = q.emit("tensor", lambda e, b=bi, c=cc, p=pb: e.transpose(
                        p[:], xn_t[b][:, 128 * c:128 * (c + 1)], ident[:]), "T", 1)
                    q.wait("vector", ttl)
                    dst = 2049 * cc + 1 + 128 * k
                    tv2 = q.emit("vector", lambda e, p=pb, d=dst: e.tensor_copy(
                        xnbf[:, d:d + 128], p[:]), "V", 1)
                    tk[("tb", cc % 2)] = tv2
                    tk[("xnbf", region, k, cc)] = tv2
                tk[("xnt_free", bi)] = ttl
            for cc in range(NCH):
                for k in range(NCK):
                    q.wait("sync", tk[("xnbf", region, k, cc)])
                if region == 0:
                    q.wait("sync", t_halo)
                    td = q.emit("sync", ld(
                        xnf[cc, :, 0:2049],
                        xnbf[:, 2049 * cc:2049 * (cc + 1)]), "D", 16)
                else:
                    td = q.emit("sync", ld(
                        xnf[cc, :, 2049:2049 + TM],
                        xnbf[:, 2049 * cc + 1:2049 * (cc + 1)]), "D", 16)
                tk[("xnf_store", region, cc)] = td

        emit_ln_region(0, xp)
        if MKDBG:
            q.wait("sync", q.tok("V"))
            q.wait("sync", q.tok("S"))
            q.emit("sync", ld(dbg3[:, 0:16], ssum[:]), "O", 16)
            q.emit("sync", ld(dbg3[:, 16:32], ssq[:]), "O", 16)
            q.emit("sync", ld(dbg3[:, 32:48], sa[:]), "O", 16)
            dmp = q.emit("sync", ld(dbg3[:, 48:64], sb_[:]), "O", 16)
            q.wait("vector", dmp)
            q.wait("scalar", dmp)
            q.wait("gpsimd", dmp)
        for cc in range(NCH):
            q.wait("vector", tk[("xnf_store", 0, cc)])
        emit_ln_region(1, xm)
        t_xn_ready = q.tok("V")

        q.wait("tensor", t_ffnc)
        for k in range(NCK):
            q.wait("tensor", tk.get(("gate_evac", )))
            tt = None
            for ci in range(NCH):
                tt = q.emit("tensor", lambda e, kk=k, c=ci: e.matmul(
                    ptb[:, 0:8],
                    xnbf[:, 2049 * c + 1 + 128 * kk:2049 * c + 1 + 128 * (kk + 1)],
                    gws[:, 8 * c:8 * (c + 1)],
                    start=(c == 0), stop=(c == NCH - 1)),
                    "T" if ci == NCH - 1 else None, 1)
            q.wait("vector", tt)
            tv = q.emit("vector", lambda e, kk=k: e.tensor_add(
                scores[:, 8 * kk:8 * (kk + 1)], ptb[:, 0:8], gbr[:]), "V", 1)
            tk[("gate_evac",)] = tv
        for k in range(NCK):
            q.emit("vector", lambda e, kk=k: e.reduce_max(
                cm1[:, kk:kk + 1], scores[:, 8 * kk:8 * (kk + 1)], axis=AXX))
        for k in range(NCK):
            q.emit("vector", lambda e, kk=k: e.tensor_scalar(
                cge1[:, 8 * kk:8 * (kk + 1)], scores[:, 8 * kk:8 * (kk + 1)],
                cm1[:, kk:kk + 1], None, ALU.is_ge))
        for k in range(NCK):
            q.emit("vector", lambda e, kk=k: e.scalar_tensor_tensor(
                cs2[:, 8 * kk:8 * (kk + 1)], cge1[:, 8 * kk:8 * (kk + 1)],
                cst[:, 3:4], scores[:, 8 * kk:8 * (kk + 1)], ALU.mult, ALU.add))
        for k in range(NCK):
            q.emit("vector", lambda e, kk=k: e.reduce_max(
                cm2[:, kk:kk + 1], cs2[:, 8 * kk:8 * (kk + 1)], axis=AXX))
        for k in range(NCK):
            q.emit("vector", lambda e, kk=k: e.tensor_scalar(
                cge1[:, 8 * kk:8 * (kk + 1)], scores[:, 8 * kk:8 * (kk + 1)],
                cm2[:, kk:kk + 1], None, ALU.is_ge))
        q.emit("vector", lambda e: e.tensor_mul(comb[:], scores[:], cge1[:]))
        tcomb = q.emit("vector", lambda e: e.tensor_copy(combbf[:], comb[:]),
                       "V", 1)
        for k in range(NCK):
            pb = pt6b if k % 2 == 0 else pt7b
            q.wait("tensor", tcomb)
            q.wait("tensor", tk.get(("tb", k % 2)))
            tt = q.emit("tensor", lambda e, kk=k, p=pb: e.transpose(
                p[0:8, 0:128], combbf[:, 8 * kk:8 * (kk + 1)], ident[:]), "T", 1)
            q.wait("vector", tt)
            tv = q.emit("vector", lambda e, kk=k, p=pb: e.tensor_copy(
                combT[:, 128 * kk:128 * (kk + 1)], p[0:8, 0:128]), "V", 1)
            tk[("tb", k % 2)] = tv
            tk[("combT", k)] = tv
        for k in range(NCK):
            for coh in range(2):
                bi = coh
                bank = ps45[bi]
                q.wait("tensor", tk[("combT", k)])
                q.wait("tensor", tk.get(("oinit", bi)))
                tt = q.emit("tensor", lambda e, kk=k, ch=coh, bk=bank: e.matmul(
                    bk[:, 0:512], combT[:, 128 * kk:128 * (kk + 1)],
                    b2s[:, 512 * ch:512 * (ch + 1)], start=True, stop=True),
                    "T", 1)
                q.wait("vector", tt)
                tv = q.emit("vector", lambda e, kk=k, ch=coh, bk=bank:
                            e.tensor_copy(
                    out_sb[:, 1024 * kk + 512 * ch:1024 * kk + 512 * (ch + 1)],
                    bk[:, 0:512]), "V", 1)
                tk[("oinit", bi)] = tv
        t_oinit = q.tok("V")

        def load_expert(e_idx, buf):
            tt = None
            for cc in range(NCH):
                q.emit("gpsimd", ld(
                    w1s[buf][:, 1024 * cc:1024 * (cc + 1)],
                    w1h[e_idx, 128 * cc:128 * (cc + 1), :]), "G", 16)
                tt = q.emit("gpsimd", ld(
                    w2s[buf][:, 1024 * cc:1024 * (cc + 1)],
                    w2h[e_idx, 128 * cc:128 * (cc + 1), :]), "G", 16)
            return tt

        wl = {0: load_expert(0, 0)}
        hq_banks = [pp[0], pp[1], sc_ps]
        q.wait("tensor", t_xn_ready)
        for ei in range(NE):
            buf = ei % 2
            if ei + 1 < NE:
                q.wait("gpsimd", tk.get(("wfree", (ei + 1) % 2)))
                wl[ei + 1] = load_expert(ei + 1, (ei + 1) % 2)
            q.wait("tensor", wl[ei])
            for t4 in range(4):
                hbuf = (ei * 4 + t4) % 2
                q.wait("scalar", tk.get(("hsb_free", hbuf)))
                relu_last = None
                for hoc in range(NCH):
                    bk = hq_banks[hoc % 3]
                    q.wait("tensor", tk.get(("hbank", hoc % 3)))
                    tt = None
                    for ci in range(NCH):
                        tt = q.emit("tensor", lambda e, c=ci, hc=hoc, tt4=t4,
                                    b=buf, bb=bk: e.matmul(
                            bb[:, 0:512],
                            w1s[b][:, 1024 * c + 128 * hc:
                                     1024 * c + 128 * (hc + 1)],
                            xnbf[:, 2049 * c + 1 + 512 * tt4:
                                 2049 * c + 1 + 512 * (tt4 + 1)],
                            start=(c == 0), stop=(c == NCH - 1)),
                            "T" if ci == NCH - 1 else None, 1)
                    q.wait("scalar", tt)
                    ts = q.emit("scalar", lambda e, hc=hoc, hb=hbuf, ee=ei,
                                bb=bk: e.activation(
                        h_sb[hb][:, 512 * hc:512 * (hc + 1)], bb[:, 0:512],
                        AF.Relu, bias=b1c[:, 8 * ee + hc:8 * ee + hc + 1]),
                        "S", 1)
                    tk[("hbank", hoc % 3)] = ts
                    relu_last = ts
                for ts4 in range(4):
                    kk = 4 * t4 + ts4
                    for coh in range(2):
                        bi = coh
                        bank = ps45[bi]
                        q.wait("tensor", relu_last)
                        q.wait("tensor", tk.get(("oute", bi), t_oinit))
                        tt = None
                        for hoc in range(NCH):
                            tt = q.emit("tensor", lambda e, hc=hoc, tts=ts4,
                                        ch=coh, hb=hbuf, b=buf, bk=bank: e.matmul(
                                bk[:, 0:512],
                                h_sb[hb][:, 512 * hc + 128 * tts:
                                          512 * hc + 128 * (tts + 1)],
                                w2s[b][:, 1024 * hc + 512 * ch:
                                         1024 * hc + 512 * (ch + 1)],
                                start=(hc == 0), stop=(hc == NCH - 1)),
                                "T" if hoc == NCH - 1 else None, 1)
                        q.wait("vector", tt)
                        tv = q.emit("vector", lambda e, k2=kk, ch=coh, ee=ei,
                                    bk=bank: e.scalar_tensor_tensor(
                            out_sb[:, 1024 * k2 + 512 * ch:
                                     1024 * k2 + 512 * (ch + 1)],
                            bk[:, 0:512], comb[:, 8 * k2 + ee:8 * k2 + ee + 1],
                            out_sb[:, 1024 * k2 + 512 * ch:
                                     1024 * k2 + 512 * (ch + 1)],
                            ALU.mult, ALU.add), "V", 1)
                        tk[("oute", bi)] = tv
                tk[("hsb_free", hbuf)] = q.tok("T")
            tk[("wfree", buf)] = q.tok("T")
        t_ffn_done = q.tok("V")
        for k in range(NCK):
            bi = k % 2
            q.wait("scalar", t_ffn_done)
            q.wait("scalar", tk.get(("g2_free", bi)))
            ts = q.emit("scalar", lambda e, kk=k, b=bi: e.activation(
                g2t[b][:], out_sb[:, 1024 * kk:1024 * (kk + 1)], AF.Silu),
                "S", 1)
            q.wait("sync", ts)
            td = q.emit("sync", ld(g2d[k, :, :], g2t[bi][:]), "D", 16)
            tk[("g2_free", bi)] = td

        q.wait("sync", q.tok("D"))
        q.wait("gpsimd", q.tok("G"))
        q.wait("gpsimd", q.tok("D"))
        q.wait("tensor", tk.get(("tb", 0)))
        q.wait("tensor", tk.get(("tb", 1)))
        q.emit("tensor", lambda e: e.transpose(pt6b[:], zer[:], ident[:]), "B", 1)
        q.emit("vector", lambda e: e.tensor_copy(stmp[:, 0:1], cst[:, 0:1]),
               "B", 1)
        q.emit("scalar", lambda e: e.activation(
            stmp[:, 1:2], cst[:, 0:1], AF.Identity), "B", 1)
        q.emit("gpsimd", lambda e: e.memset(stmp[:, 2:3], 0.0), "B", 1)
        for eng in Q.ENGS:
            q.wait(eng, ("B", 4))
        p1.close()

        MK_PHASE = int(os.environ.get("MK_PHASE", "2"))
        MK_SBS = int(os.environ.get("MK_SBS", "8"))
        STG = float(os.environ.get("MK_STG", "99"))
        p2 = ExitStack()
        pc = p2.enter_context
        ws = {}
        for nm in ("wk", "wv", "wq", "wg", "wo"):
            ws[nm] = pc(nc.sbuf_tensor(f"{nm}s", [128, NCH * 1024], BF16))
        xnsb0 = pc(nc.sbuf_tensor("xnsb0", [128, NCH * 513], BF16))
        xnsb = [xnsb0, xnsb0]
        xqbf = pc(nc.sbuf_tensor("xqbf", [128, NCH * 512], BF16))
        dtmp = pc(nc.sbuf_tensor("dtmp", [128, 2 * 512], F32))
        projf = pc(nc.sbuf_tensor("projf", [128, NCH * 512], F32))
        qsq = pc(nc.sbuf_tensor("qsq", [128, NCH * 512], BF16))
        qexp = pc(nc.sbuf_tensor("qexp", [128, NCH * 512], BF16))
        kexp = pc(nc.sbuf_tensor("kexp", [128, NCH * 512], BF16))
        k2t = pc(nc.sbuf_tensor("k2t", [128, 4 * 1024], BF16))
        vsb = pc(nc.sbuf_tensor("vsb", [128, 4 * 1024], BF16))
        gsb = pc(nc.sbuf_tensor("gsb", [128, 4 * 1024], BF16))
        At = pc(nc.sbuf_tensor("At", [128, 4 * 514], F32))
        Abf = pc(nc.sbuf_tensor("Abf", [128, 4 * 514], BF16))
        tq2 = pc(nc.sbuf_tensor("tq2", [128, NCH * 128], BF16))
        tkt = pc(nc.sbuf_tensor("tkt", [128, NCH * 128], BF16))
        trt = pc(nc.sbuf_tensor("trt", [128, NCH * 128], BF16))
        tk2f = pc(nc.sbuf_tensor("tk2f", [128, 128], BF16))
        tk2t = pc(nc.sbuf_tensor("tk2t", [128, 1024], BF16))
        w128 = pc(nc.sbuf_tensor("w128", [128, 8], F32))
        lw = pc(nc.sbuf_tensor("lw", [128, 8], F32))
        pats = pc(nc.sbuf_tensor("pats_s", [128, 512], F32))
        q2b = pc(nc.sbuf_tensor("q2b", [128, 2 * 256], BF16))
        ktb = pc(nc.sbuf_tensor("ktb", [128, 2 * 256], BF16))
        qrb = pc(nc.sbuf_tensor("qrb", [128, 2 * 640], BF16))
        ptm = [pc(nc.sbuf_tensor(f"ptm{i}", [128, 128], BF16)) for i in range(2)]
        attk = pc(nc.sbuf_tensor("attk", [128, 1024], BF16))
        attf = attk
        rzc = pc(nc.sbuf_tensor("rzc", [128, 4], F32))
        ssr = pc(nc.sbuf_tensor("ssr", [1, 512], F32))
        rrow = pc(nc.sbuf_tensor("rrow", [1, 512], BF16))
        g2l0 = pc(nc.sbuf_tensor("g2l0", [128, C], BF16))
        xtl0 = pc(nc.sbuf_tensor("xtl0", [128, C], F32))
        otl0 = pc(nc.sbuf_tensor("otl0", [128, C], F32))
        g2l = [g2l0, g2l0]
        xtl = [xtl0, xtl0]
        otl = [otl0, otl0]
        tmpf = dtmp

        if MK_PHASE >= 2:
            q.emit("gpsimd", ld(pats[:], patsh[:]), "G", 16)
        t_pats = q.tok("G")
        for nm in ("wk", "wv", "wq", "wg", "wo"):
            if MK_PHASE >= 2:
                for cc in range(NCH):
                    q.emit("gpsimd", ld(
                        ws[nm][:, 1024 * cc:1024 * (cc + 1)],
                        wts[nm][128 * cc:128 * (cc + 1), :]), "G", 16)
            tk[("wld", nm)] = q.tok("G")

        if MK_PHASE < 2:
            ts = None
        ts = q.emit("scalar", lambda e: e.activation(lw[:], td8[:], AF.Exp),
                    "S", 1) if MK_PHASE >= 2 else None
        q.wait("vector", ts)
        tv = q.emit("vector", lambda e: e.tensor_scalar(
            lw[:], lw[:], cst[:, 5:6], None, ALU.mult), "V", 1)
        q.wait("scalar", tv)
        q.wait("scalar", t_pats)
        ts_last = None
        for cc in range(NCH):
            q.emit("scalar", lambda e, c=cc: e.activation(
                tq2[:, 128 * c:128 * (c + 1)], pats[:, 0:128], AF.Exp,
                scale=lw[:, c:c + 1]))
            q.emit("scalar", lambda e, c=cc: e.activation(
                tkt[:, 128 * c:128 * (c + 1)], pats[:, 128:256], AF.Exp,
                scale=lw[:, c:c + 1]))
            ts_last = q.emit("scalar", lambda e, c=cc: e.activation(
                trt[:, 128 * c:128 * (c + 1)], pats[:, 256:384], AF.Exp,
                scale=lw[:, c:c + 1]), "S", 1)
            ts2 = q.emit("scalar", lambda e, c=cc: e.activation(
                tk2f[:], pats[:, 384:512], AF.Exp, scale=lw[:, c:c + 1]), "S", 1)
            pb = pt6b if cc % 2 == 0 else pt7b
            q.wait("tensor", ts2)
            q.wait("tensor", tk.get(("tb", cc % 2)))
            tt = q.emit("tensor", lambda e, p=pb: e.transpose(
                p[:], tk2f[:], ident[:]), "T", 1)
            q.wait("vector", tt)
            tv = q.emit("vector", lambda e, c=cc, p=pb: e.tensor_copy(
                tk2t[:, 128 * c:128 * (c + 1)], p[:]), "V", 1)
            tk[("tb", cc % 2)] = tv
            q.wait("scalar", tt)
        t_w128 = q.emit("scalar", lambda e: e.activation(
            w128[:], lw[:], AF.Exp, scale=cst[:, 6:7]), "S", 1)
        q.wait("vector", ts_last)
        q.emit("vector", lambda e: e.tensor_scalar(
            tkt[:], tkt[:], cst[:, 4:5], None, ALU.min))
        t_tbl = q.emit("vector", lambda e: e.tensor_scalar(
            trt[:], trt[:], cst[:, 4:5], None, ALU.min), "V", 1)
        if os.environ.get("MK_SCRUB"):
            for buf in (qexp, kexp, k2t, vsb, gsb, attk, xqbf, projf, qsq,
                        q2b, ktb, qrb, rzc, ssr, rrow, Abf, dtmp, tmpf,
                        xnsb0, g2l0, xtl0, otl0):
                q.emit("gpsimd", lambda e, b=buf: e.memset(b[:], 0.0))
        tA0 = q.emit("gpsimd", lambda e: e.memset(At[:], 0.0), "GM", 1)
        q.wait("vector", tA0)
        q.wait("sync", q.tok("GM"))

        proj_specs = {
            "k": ("wk", True, 1),
            "v": ("wv", False, 2),
            "q": ("wq", True, 0),
            "g": ("wg", False, 3),
        }

        sc_guard = None
        sel_guard = None
        ad_guard = None
        last_pt_tok = {}
        abf_free = None
        t_sc = None
        t_epi = {}
        xnsb_free = {}
        t_prev_sb_T = None
        t_exp_done = {}
        t_k2_done = None
        t_v_done = None
        t_g_done = None

        for s in range(MK_SBS if MK_PHASE >= 2 else 0):
            prefix = s < 4
            sbi = s % 2
            off = 512 * s if prefix else 2048 + 512 * (s - 4)
            q.wait("sync", xnsb_free.get(sbi))
            xl = None
            for cc in range(NCH):
                xl = q.emit("sync", ld(xnsb[sbi][:, 513 * cc:513 * (cc + 1)],
                                       xnf[cc, :, off:off + 513]), "D", 16)
            t_xn_sb = xl

            projs = ("k", "v") if prefix else ("k", "v", "q", "g")
            if STG < 2:
                projs = ()
            t_prev_mm = tk.get(("xqbf_free",))
            t_mix = None
            for pname in projs:
                wnm, fmaj, mixc = proj_specs[pname]
                q.wait("vector", t_xn_sb)
                q.wait("vector", t_prev_mm)
                for cc in range(NCH + 1):
                    if cc < NCH:
                        q.emit("vector", lambda e, c=cc, b=sbi: e.tensor_sub(
                            dtmp[:, 512 * (c % 2):512 * (c % 2) + 512],
                            xnsb[b][:, 513 * c + 1:513 * (c + 1)],
                            xnsb[b][:, 513 * c:513 * c + 512]))
                    if cc >= 1:
                        c0 = cc - 1
                        t_mix = q.emit("vector",
                                       lambda e, c=c0, b=sbi, m=mixc:
                                       e.scalar_tensor_tensor(
                            xqbf[:, 512 * c:512 * (c + 1)],
                            dtmp[:, 512 * (c % 2):512 * (c % 2) + 512],
                            mix8[:, 8 * m + c:8 * m + c + 1],
                            xnsb[b][:, 513 * c:513 * c + 512],
                            ALU.mult, ALU.add), "V", 1)
                q.wait("tensor", t_mix)
                q.wait("tensor", tk[("wld", wnm)])
                if STG < 3:
                    t_prev_mm = t_mix
                    continue
                if fmaj:
                    evd = {}
                    for co in range(NCH):
                        bi = co % 2
                        g_ = evd.get(bi)
                        if g_ is not None:
                            q.wait("tensor", g_[0])
                            q.wait("tensor", g_[1])
                        q.wait("tensor", t_epi.get(bi))
                        tt = None
                        for ci in range(NCH):
                            tt = q.emit("tensor", lambda e, c=ci, o=co, w=wnm:
                                        e.matmul(
                                pp[o % 2][:, 0:512],
                                ws[w][:, 1024 * c + 128 * o:
                                      1024 * c + 128 * (o + 1)],
                                xqbf[:, 512 * c:512 * (c + 1)],
                                start=(c == 0), stop=(c == NCH - 1)),
                                "T" if ci == NCH - 1 else None, 1)
                        if STG >= 3.25:
                            q.wait("vector", tt)
                            tv = q.emit("vector", lambda e, o=co: e.tensor_copy(
                                projf[:, 512 * o:512 * (o + 1)],
                                pp[o % 2][:, 0:512]), "V", 1)
                        else:
                            tv = tt
                        if STG >= 3.5:
                            q.wait("scalar", tv)
                            tsq = q.emit("scalar", lambda e, o=co: e.activation(
                                qsq[:, 512 * o:512 * (o + 1)],
                                projf[:, 512 * o:512 * (o + 1)],
                                AF.Square), "S", 1)
                        else:
                            tsq = tt
                        evd[bi] = (tv, tsq)
                    t_proj_T = q.tok("T")
                    if STG < 4:
                        t_prev_mm = t_proj_T
                        continue
                    q.wait("tensor", tsq)
                    q.wait("tensor", tv)
                    q.wait("tensor", tk.get(("ssfree",)))
                    q.wait("tensor", ad_guard)
                    tt = None
                    for ci in range(NCH):
                        tt = q.emit("tensor", lambda e, c=ci: e.matmul(
                            ps45[1][0:1, 0:512], onesc[:],
                            qsq[:, 512 * c:512 * (c + 1)],
                            start=(c == 0), stop=(c == NCH - 1)),
                            "T" if ci == NCH - 1 else None, 1)
                    q.wait("scalar", tt)
                    q.emit("scalar", lambda e: e.activation(
                        ssr[0:1, :], ps45[1][0:1, 0:512], AF.Ln,
                        scale=cst[0:1, 2:3], bias=cst[0:1, 0:1]))
                    ts = q.emit("scalar", lambda e: e.activation(
                        rrow[0:1, :], ssr[0:1, :], AF.Exp,
                        scale=cst[0:1, 1:2]), "S", 1)
                    q.wait("tensor", ts)
                    tt = q.emit("tensor", lambda e: e.matmul(
                        ps45[0][:, 0:512], onesr[:], rrow[0:1, :],
                        start=True, stop=True), "T", 1)
                    dst = qexp if pname == "q" else kexp
                    exp_toks = []
                    tv = None
                    for cc in range(NCH):
                        q.wait("vector", tt)
                        tv = q.emit("vector", lambda e, c=cc: e.tensor_mul(
                            projf[:, 512 * c:512 * (c + 1)],
                            projf[:, 512 * c:512 * (c + 1)],
                            ps45[0][:, 0:512]), "V", 1)
                        q.wait("scalar", tv)
                        ts = q.emit("scalar", lambda e, c=cc, d=dst: e.activation(
                            d[:, 512 * c:512 * (c + 1)],
                            projf[:, 512 * c:512 * (c + 1)], AF.Exp,
                            scale=nsc8[:, c:c + 1]), "S", 1)
                        exp_toks.append(ts)
                    tk[("ssfree",)] = tv
                    t_exp_done[pname] = exp_toks[-1]
                    if pname == "k" and STG >= 5:
                        ktr_pend = [(c4, cc) for c4 in range(4)
                                    for cc in range(NCH)]
                        ktr_exp = exp_toks
                else:
                    def emit_ktr(n):
                        for _ in range(n):
                            if not ktr_pend:
                                return
                            c4_, cc_ = ktr_pend.pop(0)
                            par_ = (c4_ * NCH + cc_) % 2
                            pb_ = pt6b if par_ == 0 else pt7b
                            q.wait("tensor", ktr_exp[cc_])
                            q.wait("tensor", tk.get(("tb", par_)))
                            tt_ = q.emit("tensor",
                                         lambda e, c=cc_, c4b=c4_, p=pb_:
                                         e.transpose(
                                p[:], kexp[:, 512 * c + 128 * c4b:
                                           512 * c + 128 * (c4b + 1)],
                                ident[:]), "T", 1)
                            q.wait("vector", tt_)
                            q.wait("vector", t_tbl)
                            tv_ = q.emit("vector",
                                         lambda e, c=cc_, c4b=c4_, p=pb_:
                                         e.tensor_mul(
                                k2t[:, 1024 * c4b + 128 * c:
                                    1024 * c4b + 128 * (c + 1)],
                                p[:], tk2t[:, 128 * c:128 * (c + 1)]),
                                "V", 1)
                            tk[("tb", par_)] = tv_
                            tk[("k2tok",)] = tv_
                    evd = {}
                    for ts4 in range(4):
                        for coh in range(2):
                            bi = (2 * ts4 + coh) % 2
                            if pname == "v":
                                emit_ktr(4)
                            q.wait("tensor", evd.get(bi))
                            q.wait("tensor", t_epi.get(bi))
                            tt = None
                            for ci in range(NCH):
                                tt = q.emit("tensor",
                                            lambda e, c=ci, t4_=ts4, ch=coh,
                                            w=wnm: e.matmul(
                                    pp[(2 * t4_ + ch) % 2][:, 0:512],
                                    xqbf[:, 512 * c + 128 * t4_:
                                         512 * c + 128 * (t4_ + 1)],
                                    ws[w][:, 1024 * c + 512 * ch:
                                          1024 * c + 512 * (ch + 1)],
                                    start=(c == 0), stop=(c == NCH - 1)),
                                    "T" if ci == NCH - 1 else None, 1)
                            if pname == "v":
                                q.wait("vector", tt)
                                tv = q.emit("vector", lambda e, t4_=ts4, ch=coh:
                                            e.tensor_copy(
                                    vsb[:, 1024 * t4_ + 512 * ch:
                                        1024 * t4_ + 512 * (ch + 1)],
                                    pp[(2 * t4_ + ch) % 2][:, 0:512]), "V", 1)
                                evd[bi] = tv
                            else:
                                q.wait("scalar", tt)
                                tv = q.emit("scalar", lambda e, t4_=ts4, ch=coh:
                                            e.activation(
                                    gsb[:, 1024 * t4_ + 512 * ch:
                                        1024 * t4_ + 512 * (ch + 1)],
                                    pp[(2 * t4_ + ch) % 2][:, 0:512],
                                    AF.Sigmoid), "S", 1)
                                evd[bi] = tv
                    if pname == "v":
                        emit_ktr(99)
                        t_k2_done = tk.get(("k2tok",))
                        t_v_done = q.tok("V")
                    else:
                        t_g_done = q.tok("S")
                    t_proj_T = q.tok("T")
                t_prev_mm = t_proj_T
            xnsb_free[sbi] = t_mix
            tk[("xqbf_free",)] = t_prev_mm
            if os.environ.get("MK_TRACE_A") and s < 4:
                pass

            maskcol = pmc if prefix else onesc
            qr_off = [0, 128, 256, 352, 448, 512, 576, 608]

            def emit_ttblock(c4_, h_):
                par_ = h_ % 2
                q.wait("vector", last_pt_tok.get(par_))
                for half, cc in ((0, 2 * h_), (1, 2 * h_ + 1)):
                    q.emit("vector", lambda e, hf=half, c=cc, cc4=c4_, p=par_:
                           e.tensor_mul(
                        q2b[:, 256 * p + 128 * hf:256 * p + 128 * (hf + 1)],
                        qexp[:, 512 * c + 128 * cc4:512 * c + 128 * (cc4 + 1)],
                        tq2[:, 128 * c:128 * (c + 1)]))
                    q.emit("vector", lambda e, hf=half, c=cc, cc4=c4_, p=par_:
                           e.tensor_mul(
                        ktb[:, 256 * p + 128 * hf:256 * p + 128 * (hf + 1)],
                        kexp[:, 512 * c + 128 * cc4:512 * c + 128 * (cc4 + 1)],
                        tkt[:, 128 * c:128 * (c + 1)]))
                tok = None
                for r_ in range(4):
                    j0_ = 32 * r_
                    for half, cc in ((0, 2 * h_), (1, 2 * h_ + 1)):
                        sl_ = 640 * par_ + qr_off[2 * r_ + half]
                        tok = q.emit("vector",
                                     lambda e, c=cc, cc4=c4_, jj=j0_, s0=sl_:
                                     e.tensor_mul(
                            qrb[:, s0:s0 + 128 - jj],
                            qexp[:, 512 * c + 128 * cc4 + jj:
                                 512 * c + 128 * (cc4 + 1)],
                            trt[:, 128 * c:128 * c + 128 - jj]), "V", 1)
                return tok

            for c4 in range(4):
                if not prefix:
                    q.wait("vector", t_exp_done["q"])
                    q.wait("vector", abf_free)
                    for hh in range(H):
                        tv_abf = q.emit("vector", lambda e, h2=hh: e.tensor_copy(
                            Abf[:, 514 * h2:514 * (h2 + 1)],
                            At[:, 514 * h2:514 * (h2 + 1)]), "V", 1)
                    t_abf = tv_abf
                    tok_tt = {0: emit_ttblock(c4, 0)}
                for h in range(H):
                    cc0, cc1 = 2 * h, 2 * h + 1
                    par = h % 2
                    sc_bank = sc_ps
                    if not prefix:
                        q.wait("tensor", tok_tt[h])
                        q.wait("tensor", sel_guard)
                        tmm = None
                        for r in range(4):
                            j0 = 32 * r
                            for half in range(2):
                                sl = 640 * par + qr_off[2 * r + half]
                                tmm = q.emit("tensor",
                                             lambda e, hf=half, jj=j0, p=par,
                                             s0=sl: e.matmul(
                                    ptb[jj:jj + 32, jj:128],
                                    ktb[:, 256 * p + 128 * hf + jj:
                                        256 * p + 128 * hf + jj + 32],
                                    qrb[:, s0:s0 + 128 - jj],
                                    start=(hf == 0), stop=(hf == 1),
                                    tile_position=(0, jj)),
                                    "T" if (r == 3 and half == 1) else None, 1)
                        last_pt_tok[par] = tmm
                        if h < 3:
                            tok_tt[h + 1] = emit_ttblock(c4, h + 1)
                        pmt = ptm[(c4 * H + h) % 2]
                        q.wait("vector", tmm)
                        q.wait("vector", tk.get(("ptmfree", (c4 * H + h) % 2)))
                        tv_sel = q.emit("vector", lambda e, pm_=pmt: e.select(
                            pm_[:], tril[:], ptb[:, 0:128], zer[:]), "V", 1)
                        sel_guard = tv_sel
                        q.wait("tensor", sc_guard)
                        q.wait("tensor", t_abf)
                        for half in range(2):
                            q.emit("tensor", lambda e, hf=half, h2=h,
                                   bk=sc_bank: e.matmul(
                                bk[:, 0:257],
                                q2b[:, 256 * (h2 % 2) + 128 * hf:
                                    256 * (h2 % 2) + 128 * (hf + 1)],
                                Abf[:, 514 * h2 + 257 * hf:
                                    514 * h2 + 257 * (hf + 1)],
                                start=(hf == 0), stop=False))
                        q.wait("tensor", tv_sel)
                        q.wait("tensor", t_v_done)
                        q.emit("tensor", lambda e, pm_=pmt, hh=h, c4_=c4,
                               bk=sc_bank: e.matmul(
                            bk[:, 0:256], pm_[:],
                            vsb[:, 1024 * c4_ + 256 * hh:
                                1024 * c4_ + 256 * (hh + 1)],
                            start=False, stop=False))
                        t_sc = q.emit("tensor", lambda e, pm_=pmt, bk=sc_bank:
                                      e.matmul(
                            bk[:, 256:257], pm_[:], onesc[:],
                            start=False, stop=True), "T", 1)
                        tk[("ptmfree", (c4 * H + h) % 2)] = t_sc
                    q.wait("tensor", t_k2_done)
                    q.wait("tensor", t_v_done)
                    q.wait("tensor", ad_guard)
                    q.wait("tensor", tk.get(("ssfree",)))
                    q.wait("tensor", t_exp_done["k" if prefix else "q"])
                    t_ad = None
                    for half, cc in ((0, cc0), (1, cc1)):
                        q.emit("tensor", lambda e, hf=half, c=cc, c4_=c4, hh=h:
                               e.matmul(
                            ps45[hf][:, 0:256],
                            k2t[:, 1024 * c4_ + 128 * c:
                                1024 * c4_ + 128 * (c + 1)],
                            vsb[:, 1024 * c4_ + 256 * hh:
                                1024 * c4_ + 256 * (hh + 1)],
                            start=True, stop=False))
                        t_ad = q.emit("tensor", lambda e, hf=half, c=cc, c4_=c4,
                                      mk=maskcol: e.matmul(
                            ps45[hf][:, 256:257],
                            k2t[:, 1024 * c4_ + 128 * c:
                                1024 * c4_ + 128 * (c + 1)],
                            mk[:], start=False, stop=True), "T", 1)
                    if not prefix:
                        q.wait("vector", t_sc)
                        q.emit("vector", lambda e, hh=h, bk=sc_bank:
                               e.reciprocal(rzc[:, hh:hh + 1], bk[:, 256:257]))
                    q.wait("vector", t_ad)
                    q.wait("vector", t_w128)
                    tv_au = None
                    for half in range(2):
                        tv_au = q.emit("vector", lambda e, hf=half, hh=h:
                                       e.scalar_tensor_tensor(
                            At[:, 514 * hh + 257 * hf:
                               514 * hh + 257 * (hf + 1)],
                            At[:, 514 * hh + 257 * hf:
                               514 * hh + 257 * (hf + 1)],
                            w128[:, 2 * hh + hf:2 * hh + hf + 1],
                            ps45[hf][:, 0:257], ALU.mult, ALU.add), "V", 1)
                    ad_guard = tv_au
                    if os.environ.get("MK_SERIAL"):
                        q.wait("tensor", tv_au)
                        q.wait("scalar", tv_au)
                        q.wait("gpsimd", tv_au)
                        q.wait("sync", tv_au)
                    if not prefix:
                        q.wait("vector", t_g_done)
                        q.wait("vector", tk.get(("attk_free",)))
                        tv_att = q.emit("vector", lambda e, hh=h, c4_=c4,
                                        bk=sc_bank: e.scalar_tensor_tensor(
                            attk[:, 256 * hh:256 * (hh + 1)], bk[:, 0:256],
                            rzc[:, hh:hh + 1],
                            gsb[:, 1024 * c4_ + 256 * hh:
                                1024 * c4_ + 256 * (hh + 1)],
                            ALU.mult, ALU.mult), "V", 1)
                        sc_guard = tv_att
                if not prefix:
                    abf_free = t_sc
                if not prefix:
                    mc = 4 * (s - 4) + c4
                    eb = c4 % 2
                    q.wait("sync", t_epi.get(("xfree", eb)))
                    tx = q.emit("sync", ld(xtl[eb][:],
                                           xm[128 * mc:128 * (mc + 1), :]),
                                "D", 16)
                    tg = q.emit("sync", ld(g2l[eb][:], g2d[mc, :, :]), "D", 16)
                    t_att_all = sc_guard
                    tt = None
                    for cc in range(NCH):
                        par2 = cc % 2
                        pb = pt6b if par2 == 0 else pt7b
                        q.wait("tensor", t_att_all)
                        q.wait("tensor", tk.get(("tb", par2)))
                        tt = q.emit("tensor", lambda e, c=cc, p=pb: e.transpose(
                            p[:], attk[:, 128 * c:128 * (c + 1)], ident[:]),
                            "T", 1)
                        q.wait("vector", tt)
                        tv = q.emit("vector", lambda e, c=cc, p=pb:
                                    e.tensor_copy(
                            attf[:, 128 * c:128 * (c + 1)], p[:]), "V", 1)
                        tk[("tb", par2)] = tv
                    t_attf = q.tok("V")
                    q.wait("tensor", t_attf)
                    q.wait("tensor", tk.get(("attf_free",)))
                    q.wait("tensor", t_epi.get(0))
                    q.wait("tensor", t_epi.get(1))
                    t_wo = None
                    for coh in range(2):
                        for ci in range(NCH):
                            t_wo = q.emit("tensor", lambda e, c=ci, ch=coh:
                                          e.matmul(
                                pp[ch][:, 0:512],
                                attf[:, 128 * c:128 * (c + 1)],
                                ws["wo"][:, 1024 * c + 512 * ch:
                                         1024 * c + 512 * (ch + 1)],
                                start=(c == 0), stop=(c == NCH - 1)),
                                "T" if ci == NCH - 1 else None, 1)
                    tk[("attf_free",)] = t_wo
                    tk[("attk_free",)] = t_wo
                    q.wait("vector", t_wo)
                    q.wait("vector", tx)
                    q.wait("vector", tg)
                    q.wait("vector", t_epi.get(("ofree", eb)))
                    q.emit("vector", lambda e, b=eb: e.tensor_mul(
                        tmpf[:, 0:512], pp[0][:, 0:512], g2l[b][:, 0:512]))
                    q.emit("vector", lambda e, b=eb: e.tensor_mul(
                        tmpf[:, 512:1024], pp[1][:, 0:512], g2l[b][:, 512:1024]))
                    q.emit("vector", lambda e, b=eb: e.tensor_add(
                        otl[b][:, 0:512], tmpf[:, 0:512], xtl[b][:, 0:512]))
                    tv = q.emit("vector", lambda e, b=eb: e.tensor_add(
                        otl[b][:, 512:1024], tmpf[:, 512:1024],
                        xtl[b][:, 512:1024]), "V", 1)
                    t_epi[0] = tv
                    t_epi[1] = tv
                    t_epi[("xfree", eb)] = tv
                    q.wait("sync", tv)
                    to = q.emit("sync", ld(outh[128 * mc:128 * (mc + 1), :],
                                           otl[eb][:]), "O", 16)
                    t_epi[("ofree", eb)] = to
            if os.environ.get("MK_TRACE_A") and s < 4:
                q.wait("sync", ad_guard)
                tda = q.emit("sync", lambda e, ss=s: e.dma_start(
                    dbgC[ss, :, :], At[:]), "O", 16)
                q.wait("tensor", tda)
                q.wait("vector", tda)
            t_prev_sb_T = q.tok("T")

        if MKDBG:
            q.emit("sync", ld(dbg1[:], xnf[0, :, :]), "O", 16)
            q.emit("sync", ld(dbg2[:], g2d[0, :, :]), "O", 16)
            q.wait("sync", q.tok("V"))
            q.wait("sync", q.tok("S"))
            q.wait("sync", q.tok("T"))
            q.emit("sync", lambda e: e.dma_start(dbg4[:], At[:]), "O", 16)
            q.emit("sync", lambda e: e.dma_start(dbg5[:], qexp[:]), "O", 16)
            q.emit("sync", lambda e: e.dma_start(dbg6[:], kexp[:]), "O", 16)
            q.emit("sync", lambda e: e.dma_start(dbg7[:], attk[:]), "O", 16)
            q.emit("sync", lambda e: e.dma_start(dbg8[:], vsb[:]), "O", 16)
            q.emit("sync", lambda e: e.dma_start(dbg9[:], gsb[:]), "O", 16)
            q.emit("sync", lambda e: e.dma_start(dbgA[:], k2t[:]), "O", 16)
            q.emit("sync", lambda e: e.dma_start(dbgB[:, 0:8], w128[:]), "O", 16)
            q.emit("sync", lambda e: e.dma_start(dbgB[:, 8:16], lw[:]), "O", 16)
        q.wait("sync", q.tok("O"))
        q.wait("gpsimd", q.tok("G"))

        sems = {}
        for s_ in ("T", "V", "S", "D", "G", "GM", "O", "B"):
            sems[s_] = ec(nc.semaphore(s_))
        block = ec(nc.Block())

        def runner(eng_name):
            def body(e):
                for item in q.q[eng_name]:
                    if item[0] == "w":
                        e.wait_ge(sems[item[1]], item[2])
                    else:
                        ins = item[1](e)
                        if item[2] is not None:
                            ins.then_inc(sems[item[2]], item[3])
            return body

        block.tensor(runner("tensor"))
        block.vector(runner("vector"))
        block.scalar(runner("scalar"))
        block.sync(runner("sync"))
        block.gpsimd(runner("gpsimd"))
        p2.close()

    return nc


_prog_cache = {}


def kernel(**inputs):
    x = np.asarray(inputs["L0_attn_x"], np.float32)
    td = np.asarray(inputs["L0_attn_time_decay"], np.float32)
    nsc = np.asarray(inputs["L0_attn_norm_scale"], np.float32)
    mixes = [np.asarray(inputs[f"L0_attn_mix_{m}"], np.float32).reshape(C)
             for m in "qkvg"]
    W = {m: np.asarray(inputs[f"L0_attn_W{m}"], np.float32) for m in "qkvgo"}
    gw = np.asarray(inputs["L0_ffn_gate_w"], np.float32)
    gb = np.asarray(inputs["L0_ffn_gate_b"], np.float32)
    w1 = np.asarray(inputs["L0_ffn_w1"], np.float32)
    b1 = np.asarray(inputs["L0_ffn_b1"], np.float32)
    w2 = np.asarray(inputs["L0_ffn_w2"], np.float32)
    b2 = np.asarray(inputs["L0_ffn_b2"], np.float32)

    col8 = lambda v: np.ascontiguousarray(v.reshape(8, 128).T)
    shared = {
        "wq": np.ascontiguousarray(W["q"].T),
        "wk": np.ascontiguousarray(W["k"].T),
        "wv": np.ascontiguousarray(W["v"].T),
        "wg": np.ascontiguousarray(W["g"].T),
        "wo": np.ascontiguousarray(W["o"].T),
        "w1": np.ascontiguousarray(w1), "w2": np.ascontiguousarray(w2),
        "td8": col8(td), "nsc8": col8(nsc),
        "mix8": np.ascontiguousarray(
            np.concatenate([col8(m) for m in mixes], axis=1)),
        "gw8": np.ascontiguousarray(
            gw.reshape(8, 128, 8).transpose(1, 0, 2).reshape(128, 64)),
        "gbr": np.ascontiguousarray(
            np.broadcast_to(gb[None, :], (128, 8))).astype(np.float32),
        "b1c": np.ascontiguousarray(
            b1.reshape(8, 8, 128).transpose(2, 0, 1).reshape(128, 64)),
        "b2": np.ascontiguousarray(b2),
        "ident": np.eye(128, dtype=ml_dtypes.bfloat16),
        "tril": (np.arange(128)[None, :] >= np.arange(128)[:, None]).astype(
            np.uint8),
    }
    i_idx = np.arange(128, dtype=np.float32)
    pats = np.stack([
        i_idx + 1.0,
        CP - (i_idx % 32),
        i_idx - CP,
        127.0 - i_idx,
    ], axis=0).reshape(1, 512)
    shared["pats"] = np.ascontiguousarray(
        np.broadcast_to(pats, (128, 512))).astype(np.float32)

    if "prog" not in _prog_cache:
        _prog_cache["prog"] = build_program()
    nc = _prog_cache["prog"]

    zeros_p = np.zeros((TM, C), np.float32)
    in_maps = []
    for core in range(8):
        b_, half = core // 2, core % 2
        m = dict(shared)
        m["xm"] = np.ascontiguousarray(x[b_, TM * half:TM * (half + 1)])
        m["xp"] = np.ascontiguousarray(x[b_, 0:TM]) if half == 1 else zeros_p
        m["pmv"] = np.full((128, 1), float(half), np.float32)
        in_maps.append(m)

    res = run_bass_kernel_spmd(nc, in_maps, core_ids=list(range(8)))
    out = np.empty((B, T, C), np.float32)
    for core in range(8):
        b_, half = core // 2, core % 2
        out[b_, TM * half:TM * (half + 1)] = res.results[core]["out"]
    if os.environ.get("MK_DBG"):
        sys.modules[__name__]._dbg = [
            {k2: res.results[c].get(k2) for k2 in
             ("dbg1", "dbg2", "dbg3", "dbg4", "dbg5", "dbg6", "dbg7",
              "dbg8", "dbg9", "dbgA", "dbgB", "dbgC")}
            for c in range(8)]
    return out


# version 11 (final)
# speedup vs baseline: 1.0235x; 1.0223x over previous
import os
import sys
import numpy as np
import ml_dtypes
import concourse.bass as bass
import concourse.mybir as mybir
from concourse.bass_utils import run_bass_kernel_spmd
from contextlib import ExitStack

F32 = mybir.dt.float32
BF16 = mybir.dt.bfloat16
U8 = mybir.dt.uint8
AF = mybir.ActivationFunctionType
ALU = mybir.AluOpType
AXX = mybir.AxisListType.X

B, T, C = 4, 4096, 1024
TM = 2048
NCH = 8
H = 4
E1 = 257
NCK = 16
CP = 15.5
NE = 8


class Q:
    ENGS = ("tensor", "vector", "scalar", "sync", "gpsimd")

    def __init__(self):
        self.q = {e: [] for e in self.ENGS}
        self.cnt = {}

    def wait(self, eng, tok):
        if tok is not None:
            self.q[eng].append(("w", tok[0], tok[1]))

    def emit(self, eng, fn, inc=None, amt=1):
        self.q[eng].append(("e", fn, inc, amt))
        if inc is not None:
            self.cnt[inc] = self.cnt.get(inc, 0) + amt
            return (inc, self.cnt[inc])
        return None

    def tok(self, sem):
        return (sem, self.cnt.get(sem, 0))


def build_program():
    nc = bass.Bass("TRN2", target_bir_lowering=False, debug=False)

    xm = nc.dram_tensor("xm", [TM, C], F32, kind="ExternalInput").ap()
    xp = nc.dram_tensor("xp", [TM, C], F32, kind="ExternalInput").ap()
    wts = {}
    for nm in ("wq", "wk", "wv", "wg", "wo"):
        wts[nm] = nc.dram_tensor(nm, [C, C], F32, kind="ExternalInput").ap()
    w1h = nc.dram_tensor("w1", [NE, C, C], F32, kind="ExternalInput").ap()
    w2h = nc.dram_tensor("w2", [NE, C, C], F32, kind="ExternalInput").ap()
    td8h = nc.dram_tensor("td8", [128, 8], F32, kind="ExternalInput").ap()
    nsc8h = nc.dram_tensor("nsc8", [128, 8], F32, kind="ExternalInput").ap()
    mix8h = nc.dram_tensor("mix8", [128, 32], F32, kind="ExternalInput").ap()
    gw8h = nc.dram_tensor("gw8", [128, 64], F32, kind="ExternalInput").ap()
    gbrh = nc.dram_tensor("gbr", [128, 8], F32, kind="ExternalInput").ap()
    b1ch = nc.dram_tensor("b1c", [128, 64], F32, kind="ExternalInput").ap()
    b2h = nc.dram_tensor("b2", [8, C], F32, kind="ExternalInput").ap()
    identh = nc.dram_tensor("ident", [128, 128], BF16, kind="ExternalInput").ap()
    trilh = nc.dram_tensor("tril", [128, 128], U8, kind="ExternalInput").ap()
    patsh = nc.dram_tensor("pats", [128, 512], F32, kind="ExternalInput").ap()
    pmvh = nc.dram_tensor("pmv", [128, 1], F32, kind="ExternalInput").ap()
    outh = nc.dram_tensor("out", [TM, C], F32, kind="ExternalOutput").ap()
    xnf = nc.dram_tensor("xnf", [NCH, 128, 2 * TM + 1], BF16, kind="Internal").ap()
    g2d = nc.dram_tensor("g2d", [NCK, 128, C], BF16, kind="Internal").ap()
    MKDBG = bool(os.environ.get("MK_DBG"))
    dbg1 = nc.dram_tensor("dbg1", [128, 2 * TM + 1], BF16,
                          kind="ExternalOutput").ap() if MKDBG else None
    if MKDBG:
        dbg2 = nc.dram_tensor("dbg2", [128, C], BF16, kind="ExternalOutput").ap()
        dbg3 = nc.dram_tensor("dbg3", [128, 64], F32, kind="ExternalOutput").ap()
        dbg4 = nc.dram_tensor("dbg4", [128, 4 * 514], F32,
                              kind="ExternalOutput").ap()
        dbg5 = nc.dram_tensor("dbg5", [128, 4096], BF16,
                              kind="ExternalOutput").ap()
        dbg6 = nc.dram_tensor("dbg6", [128, 4096], BF16,
                              kind="ExternalOutput").ap()
        dbg7 = nc.dram_tensor("dbg7", [128, 1024], BF16,
                              kind="ExternalOutput").ap()
        dbg8 = nc.dram_tensor("dbg8", [128, 4096], BF16,
                              kind="ExternalOutput").ap()
        dbg9 = nc.dram_tensor("dbg9", [128, 4096], BF16,
                              kind="ExternalOutput").ap()
        dbgA = nc.dram_tensor("dbgA", [128, 4096], BF16,
                              kind="ExternalOutput").ap()
        dbgB = nc.dram_tensor("dbgB", [128, 16], F32, kind="ExternalOutput").ap()
        dbgC = nc.dram_tensor("dbgC", [4, 128, 2056], F32,
                              kind="ExternalOutput").ap()

    q = Q()
    tk = {}
    st = ExitStack()
    with st:
        ec = st.enter_context

        ident = ec(nc.sbuf_tensor("ident_s", [128, 128], BF16))
        tril = ec(nc.sbuf_tensor("tril_s", [128, 128], U8))
        cst = ec(nc.sbuf_tensor("cst", [128, 10], F32))
        mix8 = ec(nc.sbuf_tensor("mix8_s", [128, 32], F32))
        td8 = ec(nc.sbuf_tensor("td8_s", [128, 8], F32))
        nsc8 = ec(nc.sbuf_tensor("nsc8_s", [128, 8], F32))
        onesc = ec(nc.sbuf_tensor("onesc", [128, 1], BF16))
        onesr = ec(nc.sbuf_tensor("onesr", [1, 128], BF16))
        pmc = ec(nc.sbuf_tensor("pmc", [128, 1], BF16))
        zer = ec(nc.sbuf_tensor("zer", [128, 128], BF16))

        def ld(dst, src):
            return lambda e, d=dst, s=src: e.dma_start(d, s)

        for dst, src in ((ident[:], identh[:]), (tril[:], trilh[:]),
                         (mix8[:], mix8h[:]), (td8[:], td8h[:]),
                         (nsc8[:], nsc8h[:]), (pmc[:], pmvh[:])):
            q.emit("gpsimd", ld(dst, src), "G", 16)
        t_cdma = q.tok("G")
        for col, val in ((0, 1e-16), (1, -0.5), (2, 1.0 / 1024.0), (3, -1e30),
                         (4, 1e25), (5, -1.0), (6, 128.0), (7, 1e-5),
                         (8, -1.0 / 1024.0)):
            q.emit("gpsimd", lambda e, c=col, v=val: e.memset(cst[:, c:c + 1], v))
        q.emit("gpsimd", lambda e: e.memset(onesc[:], 1.0))
        q.emit("gpsimd", lambda e: e.memset(onesr[:], 1.0))
        t_cmem = q.emit("gpsimd", lambda e: e.memset(zer[:], 0.0), "GM", 1)
        for eng in ("tensor", "vector", "scalar"):
            q.wait(eng, t_cdma)
            q.wait(eng, t_cmem)

        pp = [ec(nc.psum_tensor(f"pp{i}", [128, 512], F32)) for i in range(2)]
        sc_ps = ec(nc.psum_tensor("sc_ps", [128, 512], F32))
        ps2 = [sc_ps, sc_ps]
        ps45 = [ec(nc.psum_tensor(f"ps45{i}", [128, 512], F32)) for i in range(2)]
        ptb = ec(nc.psum_tensor("ptb", [128, 128], F32))
        pt6b = ec(nc.psum_tensor("pt6b", [128, 128], BF16))
        pt7b = ec(nc.psum_tensor("pt7b", [128, 128], BF16))

        p1 = ExitStack()
        pc = p1.enter_context
        x_t = [pc(nc.sbuf_tensor(f"x_t{i}", [128, C], F32)) for i in range(2)]
        sqb0 = pc(nc.sbuf_tensor("sqb0", [128, C], F32))
        sqb = [sqb0, sqb0]
        xn_t = [pc(nc.sbuf_tensor(f"xn_t{i}", [128, C], BF16)) for i in range(2)]
        ssum = pc(nc.sbuf_tensor("ssum", [128, 16], F32))
        ssq = pc(nc.sbuf_tensor("ssq", [128, 16], F32))
        sa = pc(nc.sbuf_tensor("sa", [128, 16], F32))
        sb_ = pc(nc.sbuf_tensor("sb_", [128, 16], F32))
        stmp = pc(nc.sbuf_tensor("stmp", [128, 16], F32))
        xnbf = pc(nc.sbuf_tensor("xnbf", [128, NCH * 2049], BF16))
        w1s = [pc(nc.sbuf_tensor(f"w1s{i}", [128, NCH * 1024], BF16))
               for i in range(2)]
        w2s = [pc(nc.sbuf_tensor(f"w2s{i}", [128, NCH * 1024], BF16))
               for i in range(2)]
        h_sb = [pc(nc.sbuf_tensor(f"hsb{i}", [128, 8 * 512], BF16))
                for i in range(2)]
        out_sb = pc(nc.sbuf_tensor("out_sb", [128, NCK * 1024], F32))
        scores = pc(nc.sbuf_tensor("scores", [128, NCK * 8], F32))
        cge1 = pc(nc.sbuf_tensor("cge1", [128, NCK * 8], F32))
        cs2 = pc(nc.sbuf_tensor("cs2", [128, NCK * 8], F32))
        cm1 = pc(nc.sbuf_tensor("cm1", [128, NCK], F32))
        cm2 = pc(nc.sbuf_tensor("cm2", [128, NCK], F32))
        comb = pc(nc.sbuf_tensor("comb", [128, NCK * 8], F32))
        combbf = pc(nc.sbuf_tensor("combbf", [128, NCK * 8], BF16))
        combT = pc(nc.sbuf_tensor("combT", [8, TM], BF16))
        b2s = pc(nc.sbuf_tensor("b2s_s", [8, C], BF16))
        gws = pc(nc.sbuf_tensor("gws", [128, 64], BF16))
        gbr = pc(nc.sbuf_tensor("gbr_s", [128, 8], F32))
        b1c = pc(nc.sbuf_tensor("b1c_s", [128, 64], F32))
        g2t = [pc(nc.sbuf_tensor(f"g2t{i}", [128, C], BF16)) for i in range(2)]

        for dst, src in ((gws[:], gw8h[:]), (gbr[:], gbrh[:]),
                         (b1c[:], b1ch[:]), (b2s[:], b2h[:])):
            q.emit("gpsimd", ld(dst, src), "G", 16)
        t_ffnc = q.tok("G")
        t_halo = q.emit("gpsimd", lambda e: e.memset(
            xnbf[:].rearrange("p (c w) -> p c w", w=2049)[:, :, 0:1], 0.0), "GM", 1)

        def emit_ln_region(region, src_ap):
            q.wait("gpsimd", tk.get(("stats_done_v",)))
            q.wait("gpsimd", tk.get(("stats_done_s",)))
            tz = q.emit("gpsimd", lambda e: e.memset(ssq[:], 0.0), "GM", 1)
            q.wait("scalar", tz)
            for k in range(NCK):
                bi = k % 2
                xsf = tk.get(("xstat_free", bi))
                if xsf is not None:
                    q.wait("sync", xsf[0])
                    q.wait("sync", xsf[1])
                q.wait("sync", tk.get(("xn_free", bi)))
                tl = q.emit("sync", ld(x_t[bi][:],
                                       src_ap[128 * k:128 * (k + 1), :]), "D", 16)
                q.wait("scalar", tl)
                q.wait("scalar", tk.get(("sq_free",)))
                ts = q.emit("scalar", lambda e, b=bi, kk=k: e.activation(
                    sqb[b][:], x_t[b][:], AF.Square,
                    accum_out=ssq[:, kk:kk + 1]), "S", 1)
                tk[("sq_free",)] = ts
                q.wait("vector", tl)
                tv = q.emit("vector", lambda e, b=bi, kk=k: e.reduce_sum(
                    ssum[:, kk:kk + 1], x_t[b][:], axis=AXX), "V", 1)
                tk[("xstat_free", bi)] = (tv, ts)
            q.wait("vector", tk[("sq_free",)])
            q.emit("vector", lambda e: e.tensor_mul(stmp[:], ssum[:], ssum[:]))
            q.emit("vector", lambda e: e.tensor_scalar(
                ssum[:], ssum[:], cst[:, 2:3], None, ALU.mult))
            tv = q.emit("vector", lambda e: e.scalar_tensor_tensor(
                ssq[:], stmp[:], cst[:, 8:9], ssq[:], ALU.mult, ALU.add),
                "V", 1)
            tk[("stats_done_v",)] = tv
            q.wait("scalar", tv)
            q.emit("scalar", lambda e: e.activation(
                sa[:], ssq[:], AF.Ln, scale=cst[:, 2:3], bias=cst[:, 7:8]))
            ts = q.emit("scalar", lambda e: e.activation(
                sa[:], sa[:], AF.Exp, scale=cst[:, 1:2]), "S", 1)
            tk[("stats_done_s",)] = ts
            q.wait("vector", ts)
            tvb = q.emit("vector", lambda e: e.scalar_tensor_tensor(
                sb_[:], ssum[:], cst[:, 5:6], sa[:], ALU.mult, ALU.mult), "V", 1)
            for k in range(NCK):
                bi = k % 2
                q.wait("sync", tk.get(("xn_free", bi)))
                q.wait("sync", tvb if k < 2 else None)
                q.wait("sync", ts if k < 2 else None)
                tl = q.emit("sync", ld(x_t[bi][:],
                                       src_ap[128 * k:128 * (k + 1), :]), "D", 16)
                q.wait("scalar", tl)
                q.wait("scalar", tvb)
                q.wait("scalar", tk.get(("xnt_free", bi)))
                ts = q.emit("scalar", lambda e, b=bi, kk=k: e.activation(
                    xn_t[b][:], x_t[b][:], AF.Identity,
                    scale=sa[:, kk:kk + 1], bias=sb_[:, kk:kk + 1]), "S", 1)
                tk[("xn_free", bi)] = ts
                ttl = None
                for cc in range(NCH):
                    pb = pt6b if cc % 2 == 0 else pt7b
                    q.wait("tensor", ts)
                    q.wait("tensor", tk.get(("tb", cc % 2)))
                    ttl = q.emit("tensor", lambda e, b=bi, c=cc, p=pb: e.transpose(
                        p[:], xn_t[b][:, 128 * c:128 * (c + 1)], ident[:]), "T", 1)
                    q.wait("vector", ttl)
                    dst = 2049 * cc + 1 + 128 * k
                    tv2 = q.emit("vector", lambda e, p=pb, d=dst: e.tensor_copy(
                        xnbf[:, d:d + 128], p[:]), "V", 1)
                    tk[("tb", cc % 2)] = tv2
                    tk[("xnbf", region, k, cc)] = tv2
                tk[("xnt_free", bi)] = ttl
            for cc in range(NCH):
                for k in range(NCK):
                    q.wait("sync", tk[("xnbf", region, k, cc)])
                if region == 0:
                    q.wait("sync", t_halo)
                    td = q.emit("sync", ld(
                        xnf[cc, :, 0:2049],
                        xnbf[:, 2049 * cc:2049 * (cc + 1)]), "D", 16)
                else:
                    td = q.emit("sync", ld(
                        xnf[cc, :, 2049:2049 + TM],
                        xnbf[:, 2049 * cc + 1:2049 * (cc + 1)]), "D", 16)
                tk[("xnf_store", region, cc)] = td

        emit_ln_region(0, xp)
        if MKDBG:
            q.wait("sync", q.tok("V"))
            q.wait("sync", q.tok("S"))
            q.emit("sync", ld(dbg3[:, 0:16], ssum[:]), "O", 16)
            q.emit("sync", ld(dbg3[:, 16:32], ssq[:]), "O", 16)
            q.emit("sync", ld(dbg3[:, 32:48], sa[:]), "O", 16)
            dmp = q.emit("sync", ld(dbg3[:, 48:64], sb_[:]), "O", 16)
            q.wait("vector", dmp)
            q.wait("scalar", dmp)
            q.wait("gpsimd", dmp)
        for cc in range(NCH):
            q.wait("vector", tk[("xnf_store", 0, cc)])
        emit_ln_region(1, xm)
        t_xn_ready = q.tok("V")

        q.wait("tensor", t_ffnc)
        for k in range(NCK):
            q.wait("tensor", tk.get(("gate_evac", )))
            tt = None
            for ci in range(NCH):
                tt = q.emit("tensor", lambda e, kk=k, c=ci: e.matmul(
                    ptb[:, 0:8],
                    xnbf[:, 2049 * c + 1 + 128 * kk:2049 * c + 1 + 128 * (kk + 1)],
                    gws[:, 8 * c:8 * (c + 1)],
                    start=(c == 0), stop=(c == NCH - 1)),
                    "T" if ci == NCH - 1 else None, 1)
            q.wait("vector", tt)
            tv = q.emit("vector", lambda e, kk=k: e.tensor_add(
                scores[:, 8 * kk:8 * (kk + 1)], ptb[:, 0:8], gbr[:]), "V", 1)
            tk[("gate_evac",)] = tv
        for k in range(NCK):
            q.emit("vector", lambda e, kk=k: e.reduce_max(
                cm1[:, kk:kk + 1], scores[:, 8 * kk:8 * (kk + 1)], axis=AXX))
        for k in range(NCK):
            q.emit("vector", lambda e, kk=k: e.tensor_scalar(
                cge1[:, 8 * kk:8 * (kk + 1)], scores[:, 8 * kk:8 * (kk + 1)],
                cm1[:, kk:kk + 1], None, ALU.is_ge))
        for k in range(NCK):
            q.emit("vector", lambda e, kk=k: e.scalar_tensor_tensor(
                cs2[:, 8 * kk:8 * (kk + 1)], cge1[:, 8 * kk:8 * (kk + 1)],
                cst[:, 3:4], scores[:, 8 * kk:8 * (kk + 1)], ALU.mult, ALU.add))
        for k in range(NCK):
            q.emit("vector", lambda e, kk=k: e.reduce_max(
                cm2[:, kk:kk + 1], cs2[:, 8 * kk:8 * (kk + 1)], axis=AXX))
        for k in range(NCK):
            q.emit("vector", lambda e, kk=k: e.tensor_scalar(
                cge1[:, 8 * kk:8 * (kk + 1)], scores[:, 8 * kk:8 * (kk + 1)],
                cm2[:, kk:kk + 1], None, ALU.is_ge))
        q.emit("vector", lambda e: e.tensor_mul(comb[:], scores[:], cge1[:]))
        tcomb = q.emit("vector", lambda e: e.tensor_copy(combbf[:], comb[:]),
                       "V", 1)
        for k in range(NCK):
            pb = pt6b if k % 2 == 0 else pt7b
            q.wait("tensor", tcomb)
            q.wait("tensor", tk.get(("tb", k % 2)))
            tt = q.emit("tensor", lambda e, kk=k, p=pb: e.transpose(
                p[0:8, 0:128], combbf[:, 8 * kk:8 * (kk + 1)], ident[:]), "T", 1)
            q.wait("vector", tt)
            tv = q.emit("vector", lambda e, kk=k, p=pb: e.tensor_copy(
                combT[:, 128 * kk:128 * (kk + 1)], p[0:8, 0:128]), "V", 1)
            tk[("tb", k % 2)] = tv
            tk[("combT", k)] = tv
        for k in range(NCK):
            for coh in range(2):
                bi = coh
                bank = ps45[bi]
                q.wait("tensor", tk[("combT", k)])
                q.wait("tensor", tk.get(("oinit", bi)))
                tt = q.emit("tensor", lambda e, kk=k, ch=coh, bk=bank: e.matmul(
                    bk[:, 0:512], combT[:, 128 * kk:128 * (kk + 1)],
                    b2s[:, 512 * ch:512 * (ch + 1)], start=True, stop=True),
                    "T", 1)
                q.wait("vector", tt)
                tv = q.emit("vector", lambda e, kk=k, ch=coh, bk=bank:
                            e.tensor_copy(
                    out_sb[:, 1024 * kk + 512 * ch:1024 * kk + 512 * (ch + 1)],
                    bk[:, 0:512]), "V", 1)
                tk[("oinit", bi)] = tv
        t_oinit = q.tok("V")

        def load_expert(e_idx, buf):
            tt = None
            for cc in range(NCH):
                q.emit("gpsimd", ld(
                    w1s[buf][:, 1024 * cc:1024 * (cc + 1)],
                    w1h[e_idx, 128 * cc:128 * (cc + 1), :]), "G", 16)
                tt = q.emit("gpsimd", ld(
                    w2s[buf][:, 1024 * cc:1024 * (cc + 1)],
                    w2h[e_idx, 128 * cc:128 * (cc + 1), :]), "G", 16)
            return tt

        wl = {0: load_expert(0, 0)}
        hq_banks = [pp[0], pp[1], sc_ps]
        q.wait("tensor", t_xn_ready)
        for ei in range(NE):
            buf = ei % 2
            if ei + 1 < NE:
                q.wait("gpsimd", tk.get(("wfree", (ei + 1) % 2)))
                wl[ei + 1] = load_expert(ei + 1, (ei + 1) % 2)
            q.wait("tensor", wl[ei])
            for t4 in range(4):
                hbuf = (ei * 4 + t4) % 2
                q.wait("scalar", tk.get(("hsb_free", hbuf)))
                relu_last = None
                for hoc in range(NCH):
                    bk = hq_banks[hoc % 3]
                    q.wait("tensor", tk.get(("hbank", hoc % 3)))
                    tt = None
                    for ci in range(NCH):
                        tt = q.emit("tensor", lambda e, c=ci, hc=hoc, tt4=t4,
                                    b=buf, bb=bk: e.matmul(
                            bb[:, 0:512],
                            w1s[b][:, 1024 * c + 128 * hc:
                                     1024 * c + 128 * (hc + 1)],
                            xnbf[:, 2049 * c + 1 + 512 * tt4:
                                 2049 * c + 1 + 512 * (tt4 + 1)],
                            start=(c == 0), stop=(c == NCH - 1)),
                            "T" if ci == NCH - 1 else None, 1)
                    q.wait("scalar", tt)
                    ts = q.emit("scalar", lambda e, hc=hoc, hb=hbuf, ee=ei,
                                bb=bk: e.activation(
                        h_sb[hb][:, 512 * hc:512 * (hc + 1)], bb[:, 0:512],
                        AF.Relu, bias=b1c[:, 8 * ee + hc:8 * ee + hc + 1]),
                        "S", 1)
                    tk[("hbank", hoc % 3)] = ts
                    relu_last = ts
                for ts4 in range(4):
                    kk = 4 * t4 + ts4
                    for coh in range(2):
                        bi = coh
                        bank = ps45[bi]
                        q.wait("tensor", relu_last)
                        q.wait("tensor", tk.get(("oute", bi), t_oinit))
                        tt = None
                        for hoc in range(NCH):
                            tt = q.emit("tensor", lambda e, hc=hoc, tts=ts4,
                                        ch=coh, hb=hbuf, b=buf, bk=bank: e.matmul(
                                bk[:, 0:512],
                                h_sb[hb][:, 512 * hc + 128 * tts:
                                          512 * hc + 128 * (tts + 1)],
                                w2s[b][:, 1024 * hc + 512 * ch:
                                         1024 * hc + 512 * (ch + 1)],
                                start=(hc == 0), stop=(hc == NCH - 1)),
                                "T" if hoc == NCH - 1 else None, 1)
                        q.wait("vector", tt)
                        tv = q.emit("vector", lambda e, k2=kk, ch=coh, ee=ei,
                                    bk=bank: e.scalar_tensor_tensor(
                            out_sb[:, 1024 * k2 + 512 * ch:
                                     1024 * k2 + 512 * (ch + 1)],
                            bk[:, 0:512], comb[:, 8 * k2 + ee:8 * k2 + ee + 1],
                            out_sb[:, 1024 * k2 + 512 * ch:
                                     1024 * k2 + 512 * (ch + 1)],
                            ALU.mult, ALU.add), "V", 1)
                        tk[("oute", bi)] = tv
                tk[("hsb_free", hbuf)] = q.tok("T")
            tk[("wfree", buf)] = q.tok("T")
        t_ffn_done = q.tok("V")
        for k in range(NCK):
            bi = k % 2
            q.wait("scalar", t_ffn_done)
            q.wait("scalar", tk.get(("g2_free", bi)))
            ts = q.emit("scalar", lambda e, kk=k, b=bi: e.activation(
                g2t[b][:], out_sb[:, 1024 * kk:1024 * (kk + 1)], AF.Silu),
                "S", 1)
            q.wait("sync", ts)
            td = q.emit("sync", ld(g2d[k, :, :], g2t[bi][:]), "D", 16)
            tk[("g2_free", bi)] = td

        q.wait("sync", q.tok("D"))
        q.wait("gpsimd", q.tok("G"))
        q.wait("gpsimd", q.tok("D"))
        q.wait("tensor", tk.get(("tb", 0)))
        q.wait("tensor", tk.get(("tb", 1)))
        q.emit("tensor", lambda e: e.transpose(pt6b[:], zer[:], ident[:]), "B", 1)
        q.emit("vector", lambda e: e.tensor_copy(stmp[:, 0:1], cst[:, 0:1]),
               "B", 1)
        q.emit("scalar", lambda e: e.activation(
            stmp[:, 1:2], cst[:, 0:1], AF.Identity), "B", 1)
        q.emit("gpsimd", lambda e: e.memset(stmp[:, 2:3], 0.0), "B", 1)
        for eng in Q.ENGS:
            q.wait(eng, ("B", 4))
        p1.close()

        MK_PHASE = int(os.environ.get("MK_PHASE", "2"))
        MK_SBS = int(os.environ.get("MK_SBS", "8"))
        STG = float(os.environ.get("MK_STG", "99"))
        p2 = ExitStack()
        pc = p2.enter_context
        ws = {}
        for nm in ("wk", "wv", "wq", "wg", "wo"):
            ws[nm] = pc(nc.sbuf_tensor(f"{nm}s", [128, NCH * 1024], BF16))
        xnsb0 = pc(nc.sbuf_tensor("xnsb0", [128, NCH * 513], BF16))
        xnsb = [xnsb0, xnsb0]
        xqbf = pc(nc.sbuf_tensor("xqbf", [128, NCH * 512], BF16))
        dtmp = pc(nc.sbuf_tensor("dtmp", [128, 2 * 512], F32))
        projf = pc(nc.sbuf_tensor("projf", [128, NCH * 512], F32))
        qsq = pc(nc.sbuf_tensor("qsq", [128, NCH * 512], BF16))
        qexp = pc(nc.sbuf_tensor("qexp", [128, NCH * 512], BF16))
        kexp = pc(nc.sbuf_tensor("kexp", [128, NCH * 512], BF16))
        k2t = pc(nc.sbuf_tensor("k2t", [128, 4 * 1024], BF16))
        vsb = pc(nc.sbuf_tensor("vsb", [128, 4 * 1024], BF16))
        gsb = pc(nc.sbuf_tensor("gsb", [128, 4 * 1024], BF16))
        At = pc(nc.sbuf_tensor("At", [128, 4 * 514], F32))
        Abf = pc(nc.sbuf_tensor("Abf", [128, 4 * 514], BF16))
        tq2 = pc(nc.sbuf_tensor("tq2", [128, NCH * 128], BF16))
        tkt = pc(nc.sbuf_tensor("tkt", [128, NCH * 128], BF16))
        trt = pc(nc.sbuf_tensor("trt", [128, NCH * 128], BF16))
        tk2f = pc(nc.sbuf_tensor("tk2f", [128, 128], BF16))
        tk2t = pc(nc.sbuf_tensor("tk2t", [128, 1024], BF16))
        w128 = pc(nc.sbuf_tensor("w128", [128, 8], F32))
        lw = pc(nc.sbuf_tensor("lw", [128, 8], F32))
        pats = pc(nc.sbuf_tensor("pats_s", [128, 512], F32))
        q2b = pc(nc.sbuf_tensor("q2b", [128, 2 * 256], BF16))
        ktb = pc(nc.sbuf_tensor("ktb", [128, 2 * 256], BF16))
        qrb = pc(nc.sbuf_tensor("qrb", [128, 2 * 640], BF16))
        ptm = [pc(nc.sbuf_tensor(f"ptm{i}", [128, 128], BF16)) for i in range(2)]
        attk = pc(nc.sbuf_tensor("attk", [128, 1024], BF16))
        attf = attk
        rzc = pc(nc.sbuf_tensor("rzc", [128, 4], F32))
        ssr = pc(nc.sbuf_tensor("ssr", [1, 512], F32))
        rrow = pc(nc.sbuf_tensor("rrow", [1, 512], BF16))
        g2l0 = pc(nc.sbuf_tensor("g2l0", [128, C], BF16))
        xtl0 = pc(nc.sbuf_tensor("xtl0", [128, C], F32))
        otl0 = pc(nc.sbuf_tensor("otl0", [128, C], F32))
        g2l = [g2l0, g2l0]
        xtl = [xtl0, xtl0]
        otl = [otl0, otl0]
        tmpf = dtmp

        if MK_PHASE >= 2:
            q.emit("gpsimd", ld(pats[:], patsh[:]), "G", 16)
        t_pats = q.tok("G")
        for nm in ("wk", "wv", "wq", "wg", "wo"):
            if MK_PHASE >= 2:
                for cc in range(NCH):
                    q.emit("gpsimd", ld(
                        ws[nm][:, 1024 * cc:1024 * (cc + 1)],
                        wts[nm][128 * cc:128 * (cc + 1), :]), "G", 16)
            tk[("wld", nm)] = q.tok("G")

        if MK_PHASE < 2:
            ts = None
        ts = q.emit("scalar", lambda e: e.activation(lw[:], td8[:], AF.Exp),
                    "S", 1) if MK_PHASE >= 2 else None
        q.wait("vector", ts)
        tv = q.emit("vector", lambda e: e.tensor_scalar(
            lw[:], lw[:], cst[:, 5:6], None, ALU.mult), "V", 1)
        q.wait("scalar", tv)
        q.wait("scalar", t_pats)
        ts_last = None
        for cc in range(NCH):
            q.emit("scalar", lambda e, c=cc: e.activation(
                tq2[:, 128 * c:128 * (c + 1)], pats[:, 0:128], AF.Exp,
                scale=lw[:, c:c + 1]))
            q.emit("scalar", lambda e, c=cc: e.activation(
                tkt[:, 128 * c:128 * (c + 1)], pats[:, 128:256], AF.Exp,
                scale=lw[:, c:c + 1]))
            ts_last = q.emit("scalar", lambda e, c=cc: e.activation(
                trt[:, 128 * c:128 * (c + 1)], pats[:, 256:384], AF.Exp,
                scale=lw[:, c:c + 1]), "S", 1)
            ts2 = q.emit("scalar", lambda e, c=cc: e.activation(
                tk2f[:], pats[:, 384:512], AF.Exp, scale=lw[:, c:c + 1]), "S", 1)
            pb = pt6b if cc % 2 == 0 else pt7b
            q.wait("tensor", ts2)
            q.wait("tensor", tk.get(("tb", cc % 2)))
            tt = q.emit("tensor", lambda e, p=pb: e.transpose(
                p[:], tk2f[:], ident[:]), "T", 1)
            q.wait("vector", tt)
            tv = q.emit("vector", lambda e, c=cc, p=pb: e.tensor_copy(
                tk2t[:, 128 * c:128 * (c + 1)], p[:]), "V", 1)
            tk[("tb", cc % 2)] = tv
            q.wait("scalar", tt)
        t_w128 = q.emit("scalar", lambda e: e.activation(
            w128[:], lw[:], AF.Exp, scale=cst[:, 6:7]), "S", 1)
        q.wait("vector", ts_last)
        q.emit("vector", lambda e: e.tensor_scalar(
            tkt[:], tkt[:], cst[:, 4:5], None, ALU.min))
        t_tbl = q.emit("vector", lambda e: e.tensor_scalar(
            trt[:], trt[:], cst[:, 4:5], None, ALU.min), "V", 1)
        if os.environ.get("MK_SCRUB"):
            for buf in (qexp, kexp, k2t, vsb, gsb, attk, xqbf, projf, qsq,
                        q2b, ktb, qrb, rzc, ssr, rrow, Abf, dtmp, tmpf,
                        xnsb0, g2l0, xtl0, otl0):
                q.emit("gpsimd", lambda e, b=buf: e.memset(b[:], 0.0))
        tA0 = q.emit("gpsimd", lambda e: e.memset(At[:], 0.0), "GM", 1)
        q.wait("vector", tA0)
        q.wait("sync", q.tok("GM"))

        proj_specs = {
            "k": ("wk", True, 1),
            "v": ("wv", False, 2),
            "q": ("wq", True, 0),
            "g": ("wg", False, 3),
        }

        sc_guard = None
        sel_guard = None
        ad_guard = None
        last_pt_tok = {}
        abf_free = None
        t_sc = None
        t_epi = {}
        xnsb_free = {}
        t_prev_sb_T = None
        t_exp_done = {}
        t_k2_done = None
        t_v_done = None
        t_g_done = None

        for s in range(MK_SBS if MK_PHASE >= 2 else 0):
            prefix = s < 4
            sbi = s % 2
            off = 512 * s if prefix else 2048 + 512 * (s - 4)
            q.wait("sync", xnsb_free.get(sbi))
            xl = None
            for cc in range(NCH):
                xl = q.emit("sync", ld(xnsb[sbi][:, 513 * cc:513 * (cc + 1)],
                                       xnf[cc, :, off:off + 513]), "D", 16)
            t_xn_sb = xl

            projs = ("k", "v") if prefix else ("k", "v", "q", "g")
            if STG < 2:
                projs = ()
            t_prev_mm = tk.get(("xqbf_free",))
            t_mix = None
            for pname in projs:
                wnm, fmaj, mixc = proj_specs[pname]
                q.wait("vector", t_xn_sb)
                q.wait("vector", t_prev_mm)
                for cc in range(NCH + 1):
                    if cc < NCH:
                        q.emit("vector", lambda e, c=cc, b=sbi: e.tensor_sub(
                            dtmp[:, 512 * (c % 2):512 * (c % 2) + 512],
                            xnsb[b][:, 513 * c + 1:513 * (c + 1)],
                            xnsb[b][:, 513 * c:513 * c + 512]))
                    if cc >= 1:
                        c0 = cc - 1
                        t_mix = q.emit("vector",
                                       lambda e, c=c0, b=sbi, m=mixc:
                                       e.scalar_tensor_tensor(
                            xqbf[:, 512 * c:512 * (c + 1)],
                            dtmp[:, 512 * (c % 2):512 * (c % 2) + 512],
                            mix8[:, 8 * m + c:8 * m + c + 1],
                            xnsb[b][:, 513 * c:513 * c + 512],
                            ALU.mult, ALU.add), "V", 1)
                q.wait("tensor", t_mix)
                q.wait("tensor", tk[("wld", wnm)])
                if STG < 3:
                    t_prev_mm = t_mix
                    continue
                if fmaj:
                    evd = {}
                    for co in range(NCH):
                        bi = co % 2
                        g_ = evd.get(bi)
                        if g_ is not None:
                            q.wait("tensor", g_[0])
                            q.wait("tensor", g_[1])
                        q.wait("tensor", t_epi.get(bi))
                        tt = None
                        for ci in range(NCH):
                            tt = q.emit("tensor", lambda e, c=ci, o=co, w=wnm:
                                        e.matmul(
                                pp[o % 2][:, 0:512],
                                ws[w][:, 1024 * c + 128 * o:
                                      1024 * c + 128 * (o + 1)],
                                xqbf[:, 512 * c:512 * (c + 1)],
                                start=(c == 0), stop=(c == NCH - 1)),
                                "T" if ci == NCH - 1 else None, 1)
                        if STG >= 3.25:
                            q.wait("vector", tt)
                            tv = q.emit("vector", lambda e, o=co: e.tensor_copy(
                                projf[:, 512 * o:512 * (o + 1)],
                                pp[o % 2][:, 0:512]), "V", 1)
                        else:
                            tv = tt
                        if STG >= 3.5:
                            q.wait("scalar", tv)
                            tsq = q.emit("scalar", lambda e, o=co: e.activation(
                                qsq[:, 512 * o:512 * (o + 1)],
                                projf[:, 512 * o:512 * (o + 1)],
                                AF.Square), "S", 1)
                        else:
                            tsq = tt
                        evd[bi] = (tv, tsq)
                    t_proj_T = q.tok("T")
                    if STG < 4:
                        t_prev_mm = t_proj_T
                        continue
                    q.wait("tensor", tsq)
                    q.wait("tensor", tv)
                    q.wait("tensor", tk.get(("ssfree",)))
                    q.wait("tensor", ad_guard)
                    tt = None
                    for ci in range(NCH):
                        tt = q.emit("tensor", lambda e, c=ci: e.matmul(
                            ps45[1][0:1, 0:512], onesc[:],
                            qsq[:, 512 * c:512 * (c + 1)],
                            start=(c == 0), stop=(c == NCH - 1)),
                            "T" if ci == NCH - 1 else None, 1)
                    q.wait("scalar", tt)
                    q.emit("scalar", lambda e: e.activation(
                        ssr[0:1, :], ps45[1][0:1, 0:512], AF.Ln,
                        scale=cst[0:1, 2:3], bias=cst[0:1, 0:1]))
                    ts = q.emit("scalar", lambda e: e.activation(
                        rrow[0:1, :], ssr[0:1, :], AF.Exp,
                        scale=cst[0:1, 1:2]), "S", 1)
                    q.wait("tensor", ts)
                    tt = q.emit("tensor", lambda e: e.matmul(
                        ps45[0][:, 0:512], onesr[:], rrow[0:1, :],
                        start=True, stop=True), "T", 1)
                    dst = qexp if pname == "q" else kexp
                    exp_toks = []
                    tv = None
                    for cc in range(NCH):
                        q.wait("vector", tt)
                        tv = q.emit("vector", lambda e, c=cc: e.tensor_mul(
                            projf[:, 512 * c:512 * (c + 1)],
                            projf[:, 512 * c:512 * (c + 1)],
                            ps45[0][:, 0:512]), "V", 1)
                        q.wait("scalar", tv)
                        ts = q.emit("scalar", lambda e, c=cc, d=dst: e.activation(
                            d[:, 512 * c:512 * (c + 1)],
                            projf[:, 512 * c:512 * (c + 1)], AF.Exp,
                            scale=nsc8[:, c:c + 1]), "S", 1)
                        exp_toks.append(ts)
                    tk[("ssfree",)] = tv
                    t_exp_done[pname] = exp_toks[-1]
                    if pname == "k" and STG >= 5:
                        q.wait("vector", t_tbl)
                        for c4 in range(4):
                            for cc in range(NCH):
                                par = (c4 * NCH + cc) % 2
                                pb = pt6b if par == 0 else pt7b
                                q.wait("tensor", exp_toks[cc])
                                q.wait("tensor", tk.get(("tb", par)))
                                tt = q.emit("tensor",
                                            lambda e, c=cc, c4_=c4, p=pb:
                                            e.transpose(
                                    p[:], kexp[:, 512 * c + 128 * c4_:
                                               512 * c + 128 * (c4_ + 1)],
                                    ident[:]), "T", 1)
                                q.wait("vector", tt)
                                tv = q.emit("vector",
                                            lambda e, c=cc, c4_=c4, p=pb:
                                            e.tensor_mul(
                                    k2t[:, 1024 * c4_ + 128 * c:
                                        1024 * c4_ + 128 * (c + 1)],
                                    p[:], tk2t[:, 128 * c:128 * (c + 1)]),
                                    "V", 1)
                                tk[("tb", par)] = tv
                        t_k2_done = q.tok("V")
                else:
                    evd = {}
                    for ts4 in range(4):
                        for coh in range(2):
                            bi = (2 * ts4 + coh) % 2
                            q.wait("tensor", evd.get(bi))
                            q.wait("tensor", t_epi.get(bi))
                            tt = None
                            for ci in range(NCH):
                                tt = q.emit("tensor",
                                            lambda e, c=ci, t4_=ts4, ch=coh,
                                            w=wnm: e.matmul(
                                    pp[(2 * t4_ + ch) % 2][:, 0:512],
                                    xqbf[:, 512 * c + 128 * t4_:
                                         512 * c + 128 * (t4_ + 1)],
                                    ws[w][:, 1024 * c + 512 * ch:
                                          1024 * c + 512 * (ch + 1)],
                                    start=(c == 0), stop=(c == NCH - 1)),
                                    "T" if ci == NCH - 1 else None, 1)
                            if pname == "v":
                                q.wait("vector", tt)
                                tv = q.emit("vector", lambda e, t4_=ts4, ch=coh:
                                            e.tensor_copy(
                                    vsb[:, 1024 * t4_ + 512 * ch:
                                        1024 * t4_ + 512 * (ch + 1)],
                                    pp[(2 * t4_ + ch) % 2][:, 0:512]), "V", 1)
                                evd[bi] = tv
                            else:
                                q.wait("scalar", tt)
                                tv = q.emit("scalar", lambda e, t4_=ts4, ch=coh:
                                            e.activation(
                                    gsb[:, 1024 * t4_ + 512 * ch:
                                        1024 * t4_ + 512 * (ch + 1)],
                                    pp[(2 * t4_ + ch) % 2][:, 0:512],
                                    AF.Sigmoid), "S", 1)
                                evd[bi] = tv
                    if pname == "v":
                        t_v_done = q.tok("V")
                    else:
                        t_g_done = q.tok("S")
                    t_proj_T = q.tok("T")
                t_prev_mm = t_proj_T
            xnsb_free[sbi] = t_mix
            tk[("xqbf_free",)] = t_prev_mm
            if os.environ.get("MK_TRACE_A") and s < 4:
                pass

            maskcol = pmc if prefix else onesc
            qr_off = [0, 128, 256, 352, 448, 512, 576, 608]

            def emit_ttblock(c4_, h_):
                par_ = h_ % 2
                q.wait("vector", last_pt_tok.get(par_))
                for half, cc in ((0, 2 * h_), (1, 2 * h_ + 1)):
                    q.emit("vector", lambda e, hf=half, c=cc, cc4=c4_, p=par_:
                           e.tensor_mul(
                        q2b[:, 256 * p + 128 * hf:256 * p + 128 * (hf + 1)],
                        qexp[:, 512 * c + 128 * cc4:512 * c + 128 * (cc4 + 1)],
                        tq2[:, 128 * c:128 * (c + 1)]))
                    q.emit("vector", lambda e, hf=half, c=cc, cc4=c4_, p=par_:
                           e.tensor_mul(
                        ktb[:, 256 * p + 128 * hf:256 * p + 128 * (hf + 1)],
                        kexp[:, 512 * c + 128 * cc4:512 * c + 128 * (cc4 + 1)],
                        tkt[:, 128 * c:128 * (c + 1)]))
                tok = None
                for r_ in range(4):
                    j0_ = 32 * r_
                    for half, cc in ((0, 2 * h_), (1, 2 * h_ + 1)):
                        sl_ = 640 * par_ + qr_off[2 * r_ + half]
                        tok = q.emit("vector",
                                     lambda e, c=cc, cc4=c4_, jj=j0_, s0=sl_:
                                     e.tensor_mul(
                            qrb[:, s0:s0 + 128 - jj],
                            qexp[:, 512 * c + 128 * cc4 + jj:
                                 512 * c + 128 * (cc4 + 1)],
                            trt[:, 128 * c:128 * c + 128 - jj]), "V", 1)
                return tok

            for c4 in range(4):
                if not prefix:
                    q.wait("vector", t_exp_done["q"])
                    q.wait("vector", abf_free)
                    for hh in range(H):
                        tv_abf = q.emit("vector", lambda e, h2=hh: e.tensor_copy(
                            Abf[:, 514 * h2:514 * (h2 + 1)],
                            At[:, 514 * h2:514 * (h2 + 1)]), "V", 1)
                    t_abf = tv_abf
                    tok_tt = {0: emit_ttblock(c4, 0)}
                for h in range(H):
                    cc0, cc1 = 2 * h, 2 * h + 1
                    par = h % 2
                    sc_bank = sc_ps
                    if not prefix:
                        q.wait("tensor", tok_tt[h])
                        q.wait("tensor", sel_guard)
                        tmm = None
                        for r in range(4):
                            j0 = 32 * r
                            for half in range(2):
                                sl = 640 * par + qr_off[2 * r + half]
                                tmm = q.emit("tensor",
                                             lambda e, hf=half, jj=j0, p=par,
                                             s0=sl: e.matmul(
                                    ptb[jj:jj + 32, jj:128],
                                    ktb[:, 256 * p + 128 * hf + jj:
                                        256 * p + 128 * hf + jj + 32],
                                    qrb[:, s0:s0 + 128 - jj],
                                    start=(hf == 0), stop=(hf == 1),
                                    tile_position=(0, jj)),
                                    "T" if (r == 3 and half == 1) else None, 1)
                        last_pt_tok[par] = tmm
                        if h < 3:
                            tok_tt[h + 1] = emit_ttblock(c4, h + 1)
                        pmt = ptm[(c4 * H + h) % 2]
                        q.wait("vector", tmm)
                        q.wait("vector", tk.get(("ptmfree", (c4 * H + h) % 2)))
                        tv_sel = q.emit("vector", lambda e, pm_=pmt: e.select(
                            pm_[:], tril[:], ptb[:, 0:128], zer[:]), "V", 1)
                        sel_guard = tv_sel
                    q.wait("tensor", t_k2_done)
                    q.wait("tensor", t_v_done)
                    q.wait("tensor", ad_guard)
                    q.wait("tensor", tk.get(("ssfree",)))
                    q.wait("tensor", t_exp_done["k" if prefix else "q"])
                    t_ad = None
                    for half, cc in ((0, cc0), (1, cc1)):
                        q.emit("tensor", lambda e, hf=half, c=cc, c4_=c4, hh=h:
                               e.matmul(
                            ps45[hf][:, 0:256],
                            k2t[:, 1024 * c4_ + 128 * c:
                                1024 * c4_ + 128 * (c + 1)],
                            vsb[:, 1024 * c4_ + 256 * hh:
                                1024 * c4_ + 256 * (hh + 1)],
                            start=True, stop=False))
                        t_ad = q.emit("tensor", lambda e, hf=half, c=cc, c4_=c4,
                                      mk=maskcol: e.matmul(
                            ps45[hf][:, 256:257],
                            k2t[:, 1024 * c4_ + 128 * c:
                                1024 * c4_ + 128 * (c + 1)],
                            mk[:], start=False, stop=True), "T", 1)
                    if not prefix:
                        q.wait("tensor", sc_guard)
                        q.wait("tensor", t_abf)
                        for half in range(2):
                            q.emit("tensor", lambda e, hf=half, h2=h,
                                   bk=sc_bank: e.matmul(
                                bk[:, 0:257],
                                q2b[:, 256 * (h2 % 2) + 128 * hf:
                                    256 * (h2 % 2) + 128 * (hf + 1)],
                                Abf[:, 514 * h2 + 257 * hf:
                                    514 * h2 + 257 * (hf + 1)],
                                start=(hf == 0), stop=False))
                        q.wait("tensor", tv_sel)
                        q.wait("tensor", t_v_done)
                        q.emit("tensor", lambda e, pm_=pmt, hh=h, c4_=c4,
                               bk=sc_bank: e.matmul(
                            bk[:, 0:256], pm_[:],
                            vsb[:, 1024 * c4_ + 256 * hh:
                                1024 * c4_ + 256 * (hh + 1)],
                            start=False, stop=False))
                        t_sc = q.emit("tensor", lambda e, pm_=pmt, bk=sc_bank:
                                      e.matmul(
                            bk[:, 256:257], pm_[:], onesc[:],
                            start=False, stop=True), "T", 1)
                        tk[("ptmfree", (c4 * H + h) % 2)] = t_sc
                    if not prefix:
                        q.wait("vector", t_sc)
                        q.emit("vector", lambda e, hh=h, bk=sc_bank:
                               e.reciprocal(rzc[:, hh:hh + 1], bk[:, 256:257]))
                    q.wait("vector", t_ad)
                    q.wait("vector", t_w128)
                    tv_au = None
                    for half in range(2):
                        tv_au = q.emit("vector", lambda e, hf=half, hh=h:
                                       e.scalar_tensor_tensor(
                            At[:, 514 * hh + 257 * hf:
                               514 * hh + 257 * (hf + 1)],
                            At[:, 514 * hh + 257 * hf:
                               514 * hh + 257 * (hf + 1)],
                            w128[:, 2 * hh + hf:2 * hh + hf + 1],
                            ps45[hf][:, 0:257], ALU.mult, ALU.add), "V", 1)
                    ad_guard = tv_au
                    if os.environ.get("MK_SERIAL"):
                        q.wait("tensor", tv_au)
                        q.wait("scalar", tv_au)
                        q.wait("gpsimd", tv_au)
                        q.wait("sync", tv_au)
                    if not prefix:
                        q.wait("vector", t_g_done)
                        q.wait("vector", tk.get(("attk_free",)))
                        tv_att = q.emit("vector", lambda e, hh=h, c4_=c4,
                                        bk=sc_bank: e.scalar_tensor_tensor(
                            attk[:, 256 * hh:256 * (hh + 1)], bk[:, 0:256],
                            rzc[:, hh:hh + 1],
                            gsb[:, 1024 * c4_ + 256 * hh:
                                1024 * c4_ + 256 * (hh + 1)],
                            ALU.mult, ALU.mult), "V", 1)
                        sc_guard = tv_att
                if not prefix:
                    abf_free = t_sc
                if not prefix:
                    mc = 4 * (s - 4) + c4
                    eb = c4 % 2
                    q.wait("sync", t_epi.get(("xfree", eb)))
                    tx = q.emit("sync", ld(xtl[eb][:],
                                           xm[128 * mc:128 * (mc + 1), :]),
                                "D", 16)
                    tg = q.emit("sync", ld(g2l[eb][:], g2d[mc, :, :]), "D", 16)
                    t_att_all = sc_guard
                    tt = None
                    for cc in range(NCH):
                        par2 = cc % 2
                        pb = pt6b if par2 == 0 else pt7b
                        q.wait("tensor", t_att_all)
                        q.wait("tensor", tk.get(("tb", par2)))
                        tt = q.emit("tensor", lambda e, c=cc, p=pb: e.transpose(
                            p[:], attk[:, 128 * c:128 * (c + 1)], ident[:]),
                            "T", 1)
                        q.wait("vector", tt)
                        tv = q.emit("vector", lambda e, c=cc, p=pb:
                                    e.tensor_copy(
                            attf[:, 128 * c:128 * (c + 1)], p[:]), "V", 1)
                        tk[("tb", par2)] = tv
                    t_attf = q.tok("V")
                    q.wait("tensor", t_attf)
                    q.wait("tensor", tk.get(("attf_free",)))
                    q.wait("tensor", t_epi.get(0))
                    q.wait("tensor", t_epi.get(1))
                    t_wo = None
                    for coh in range(2):
                        for ci in range(NCH):
                            t_wo = q.emit("tensor", lambda e, c=ci, ch=coh:
                                          e.matmul(
                                pp[ch][:, 0:512],
                                attf[:, 128 * c:128 * (c + 1)],
                                ws["wo"][:, 1024 * c + 512 * ch:
                                         1024 * c + 512 * (ch + 1)],
                                start=(c == 0), stop=(c == NCH - 1)),
                                "T" if ci == NCH - 1 else None, 1)
                    tk[("attf_free",)] = t_wo
                    tk[("attk_free",)] = t_wo
                    q.wait("vector", t_wo)
                    q.wait("vector", tx)
                    q.wait("vector", tg)
                    q.wait("vector", t_epi.get(("ofree", eb)))
                    q.emit("vector", lambda e, b=eb: e.tensor_mul(
                        tmpf[:, 0:512], pp[0][:, 0:512], g2l[b][:, 0:512]))
                    q.emit("vector", lambda e, b=eb: e.tensor_mul(
                        tmpf[:, 512:1024], pp[1][:, 0:512], g2l[b][:, 512:1024]))
                    q.emit("vector", lambda e, b=eb: e.tensor_add(
                        otl[b][:, 0:512], tmpf[:, 0:512], xtl[b][:, 0:512]))
                    tv = q.emit("vector", lambda e, b=eb: e.tensor_add(
                        otl[b][:, 512:1024], tmpf[:, 512:1024],
                        xtl[b][:, 512:1024]), "V", 1)
                    t_epi[0] = tv
                    t_epi[1] = tv
                    t_epi[("xfree", eb)] = tv
                    q.wait("sync", tv)
                    to = q.emit("sync", ld(outh[128 * mc:128 * (mc + 1), :],
                                           otl[eb][:]), "O", 16)
                    t_epi[("ofree", eb)] = to
            if os.environ.get("MK_TRACE_A") and s < 4:
                q.wait("sync", ad_guard)
                tda = q.emit("sync", lambda e, ss=s: e.dma_start(
                    dbgC[ss, :, :], At[:]), "O", 16)
                q.wait("tensor", tda)
                q.wait("vector", tda)
            t_prev_sb_T = q.tok("T")

        if MKDBG:
            q.emit("sync", ld(dbg1[:], xnf[0, :, :]), "O", 16)
            q.emit("sync", ld(dbg2[:], g2d[0, :, :]), "O", 16)
            q.wait("sync", q.tok("V"))
            q.wait("sync", q.tok("S"))
            q.wait("sync", q.tok("T"))
            q.emit("sync", lambda e: e.dma_start(dbg4[:], At[:]), "O", 16)
            q.emit("sync", lambda e: e.dma_start(dbg5[:], qexp[:]), "O", 16)
            q.emit("sync", lambda e: e.dma_start(dbg6[:], kexp[:]), "O", 16)
            q.emit("sync", lambda e: e.dma_start(dbg7[:], attk[:]), "O", 16)
            q.emit("sync", lambda e: e.dma_start(dbg8[:], vsb[:]), "O", 16)
            q.emit("sync", lambda e: e.dma_start(dbg9[:], gsb[:]), "O", 16)
            q.emit("sync", lambda e: e.dma_start(dbgA[:], k2t[:]), "O", 16)
            q.emit("sync", lambda e: e.dma_start(dbgB[:, 0:8], w128[:]), "O", 16)
            q.emit("sync", lambda e: e.dma_start(dbgB[:, 8:16], lw[:]), "O", 16)
        q.wait("sync", q.tok("O"))
        q.wait("gpsimd", q.tok("G"))

        sems = {}
        for s_ in ("T", "V", "S", "D", "G", "GM", "O", "B"):
            sems[s_] = ec(nc.semaphore(s_))
        block = ec(nc.Block())

        def runner(eng_name):
            def body(e):
                for item in q.q[eng_name]:
                    if item[0] == "w":
                        e.wait_ge(sems[item[1]], item[2])
                    else:
                        ins = item[1](e)
                        if item[2] is not None:
                            ins.then_inc(sems[item[2]], item[3])
            return body

        block.tensor(runner("tensor"))
        block.vector(runner("vector"))
        block.scalar(runner("scalar"))
        block.sync(runner("sync"))
        block.gpsimd(runner("gpsimd"))
        p2.close()

    return nc


_prog_cache = {}


def kernel(**inputs):
    x = np.asarray(inputs["L0_attn_x"], np.float32)
    td = np.asarray(inputs["L0_attn_time_decay"], np.float32)
    nsc = np.asarray(inputs["L0_attn_norm_scale"], np.float32)
    mixes = [np.asarray(inputs[f"L0_attn_mix_{m}"], np.float32).reshape(C)
             for m in "qkvg"]
    W = {m: np.asarray(inputs[f"L0_attn_W{m}"], np.float32) for m in "qkvgo"}
    gw = np.asarray(inputs["L0_ffn_gate_w"], np.float32)
    gb = np.asarray(inputs["L0_ffn_gate_b"], np.float32)
    w1 = np.asarray(inputs["L0_ffn_w1"], np.float32)
    b1 = np.asarray(inputs["L0_ffn_b1"], np.float32)
    w2 = np.asarray(inputs["L0_ffn_w2"], np.float32)
    b2 = np.asarray(inputs["L0_ffn_b2"], np.float32)

    col8 = lambda v: np.ascontiguousarray(v.reshape(8, 128).T)
    shared = {
        "wq": np.ascontiguousarray(W["q"].T),
        "wk": np.ascontiguousarray(W["k"].T),
        "wv": np.ascontiguousarray(W["v"].T),
        "wg": np.ascontiguousarray(W["g"].T),
        "wo": np.ascontiguousarray(W["o"].T),
        "w1": np.ascontiguousarray(w1), "w2": np.ascontiguousarray(w2),
        "td8": col8(td), "nsc8": col8(nsc),
        "mix8": np.ascontiguousarray(
            np.concatenate([col8(m) for m in mixes], axis=1)),
        "gw8": np.ascontiguousarray(
            gw.reshape(8, 128, 8).transpose(1, 0, 2).reshape(128, 64)),
        "gbr": np.ascontiguousarray(
            np.broadcast_to(gb[None, :], (128, 8))).astype(np.float32),
        "b1c": np.ascontiguousarray(
            b1.reshape(8, 8, 128).transpose(2, 0, 1).reshape(128, 64)),
        "b2": np.ascontiguousarray(b2),
        "ident": np.eye(128, dtype=ml_dtypes.bfloat16),
        "tril": (np.arange(128)[None, :] >= np.arange(128)[:, None]).astype(
            np.uint8),
    }
    i_idx = np.arange(128, dtype=np.float32)
    pats = np.stack([
        i_idx + 1.0,
        CP - (i_idx % 32),
        i_idx - CP,
        127.0 - i_idx,
    ], axis=0).reshape(1, 512)
    shared["pats"] = np.ascontiguousarray(
        np.broadcast_to(pats, (128, 512))).astype(np.float32)

    if "prog" not in _prog_cache:
        _prog_cache["prog"] = build_program()
    nc = _prog_cache["prog"]

    zeros_p = np.zeros((TM, C), np.float32)
    in_maps = []
    for core in range(8):
        b_, half = core // 2, core % 2
        m = dict(shared)
        m["xm"] = np.ascontiguousarray(x[b_, TM * half:TM * (half + 1)])
        m["xp"] = np.ascontiguousarray(x[b_, 0:TM]) if half == 1 else zeros_p
        m["pmv"] = np.full((128, 1), float(half), np.float32)
        in_maps.append(m)

    res = run_bass_kernel_spmd(nc, in_maps, core_ids=list(range(8)))
    out = np.empty((B, T, C), np.float32)
    for core in range(8):
        b_, half = core // 2, core % 2
        out[b_, TM * half:TM * (half + 1)] = res.results[core]["out"]
    if os.environ.get("MK_DBG"):
        sys.modules[__name__]._dbg = [
            {k2: res.results[c].get(k2) for k2 in
             ("dbg1", "dbg2", "dbg3", "dbg4", "dbg5", "dbg6", "dbg7",
              "dbg8", "dbg9", "dbgA", "dbgB", "dbgC")}
            for c in range(8)]
    return out
